# version 05
# baseline (speedup 1.0000x reference)
import numpy as np
import ml_dtypes
from contextlib import ExitStack
import concourse.bass as bass
import concourse.mybir as mybir
from concourse.bass_utils import run_bass_kernel_spmd

F32 = mybir.dt.float32
BF16 = mybir.dt.bfloat16
U8 = mybir.dt.uint8
U16 = mybir.dt.uint16
ALU = mybir.AluOpType
AF = mybir.ActivationFunctionType
AX = mybir.AxisListType

D = 4096
DFF = 11008
S = 2048
NB_D = 32
NB_N = 86
RUNS = [18, 18, 18, 18, 14]
RUN_STARTS = [0, 18, 36, 54, 72]
SCALE = 1.0 / (128.0 ** 0.5)
NHEAD = 32
NKV = 8

_LUTLO = np.array([0, 4, 1, 5])
_LUTHI = np.array([2, 6, 3, 7])


def _canonical_perm(N):
    pi = np.empty(N, dtype=np.int64)
    a = np.arange(N // 128)
    for p in range(128):
        if p < 64:
            m = 64 * a + p
            pi[a * 128 + p] = 8 * (m // 4) + _LUTLO[p % 4]
        else:
            m = 64 * a + (p - 64)
            pi[a * 128 + p] = 8 * (m // 4) + _LUTHI[(p - 64) % 4]
    return pi


PI_D = _canonical_perm(D)
PI_N = _canonical_perm(DFF)
GPOS_32 = np.argsort(np.concatenate([np.arange(0, 32, 2), np.arange(1, 32, 2)]))
GPOS_86 = np.argsort(np.concatenate([np.arange(0, 86, 2), np.arange(1, 86, 2)]))


def _interleave_idx(L):
    ki = np.empty(L, dtype=np.int64)
    ki[0::2] = np.arange(L // 2)
    ki[1::2] = L // 2 + np.arange(L // 2)
    return ki


def _gateup_slabs(qweight):
    qb = np.ascontiguousarray(qweight).view(np.uint8).reshape(D, DFF // 2)
    qb = qb[PI_D, :][_interleave_idx(D), :]
    return np.ascontiguousarray(qb.reshape(D, NB_N, 64).transpose(1, 2, 0))


def _down_slabs(qweight):
    db = np.ascontiguousarray(qweight).view(np.uint8).reshape(DFF, D // 2)
    db = db[PI_N, :]
    out = np.empty((NB_D, 64, DFF), dtype=np.uint8)
    for r, R in enumerate(RUNS):
        lo = RUN_STARTS[r] * 128
        L = R * 128
        sub = db[lo + _interleave_idx(L), :]
        out[:, :, lo:lo + L] = sub.reshape(L, NB_D, 64).transpose(1, 2, 0)
    return np.ascontiguousarray(out)


def _gateup_scales(scales):
    sc = scales[:, PI_N]
    t = sc.T.reshape(NB_N, 128, 32).transpose(1, 0, 2).reshape(128, NB_N * 32)
    return np.ascontiguousarray(t.astype(np.float32))


def _down_scales(scales):
    sc = scales[:, PI_D]
    t = sc.T.reshape(NB_D, 128, 86).transpose(1, 0, 2).reshape(128, NB_D * 86)
    return np.ascontiguousarray(t.astype(np.float32))


def _gateup_zero_bytes(qzeros):
    zb = np.ascontiguousarray(qzeros).view(np.uint8).reshape(32, DFF // 2).T
    h = np.empty((128, NB_N * 32), dtype=np.uint8)
    for a in range(NB_N):
        blk = zb[64 * a:64 * a + 64, :]
        h[0:64, a * 32:(a + 1) * 32] = blk
        h[64:128, a * 32:(a + 1) * 32] = blk
    return h


def _down_zero_bytes(qzeros):
    zb = np.ascontiguousarray(qzeros).view(np.uint8).reshape(86, D // 2).T
    h = np.empty((128, NB_D * 86), dtype=np.uint8)
    for t in range(NB_D):
        blk = zb[64 * t:64 * t + 64, :]
        h[0:64, t * 86:(t + 1) * 86] = blk
        h[64:128, t * 86:(t + 1) * 86] = blk
    return h


def _tile_kpart(w):
    wc = w[PI_D, :]
    C = w.shape[1]
    t = wc.reshape(NB_D, 128, C // 128, 128).transpose(2, 1, 0, 3)
    return np.ascontiguousarray(t.reshape(C // 128, 128, NB_D * 128)
                                .astype(np.float32))


def _tile_wo(wo):
    wc = wo[:, PI_D]
    t = wc.reshape(32, 128, NB_D, 128).transpose(2, 1, 0, 3)
    return np.ascontiguousarray(t.reshape(NB_D, 128, 32 * 128).astype(np.float32))


ENGS = ("pe", "dve", "act", "pool", "sp")


class Step:
    __slots__ = ("eng", "val")
    def __init__(self, eng, val):
        self.eng = eng
        self.val = val


class Res:
    __slots__ = ("w", "rs")
    def __init__(self):
        self.w = None
        self.rs = []
    def rd(self):
        return [self.w] if self.w is not None else []
    def wd(self):
        d = list(self.rs)
        if self.w is not None:
            d.append(self.w)
        return d
    def read(self, s):
        self.rs.append(s)
    def wrote(self, s):
        self.w = s
        self.rs = []


class Sched:
    def __init__(self, nc, stack):
        self.nc = nc
        self.sems = {e: stack.enter_context(nc.semaphore(f"sem_{e}"))
                     for e in ENGS}
        self.counts = {e: 0 for e in ENGS}
        self.items = []
        self.covered = {e: {} for e in ENGS}

    NO_DEDUPE = True

    def _waits(self, eng, deps):
        cov = self.covered[eng]
        out = []
        seen = {}
        for d in deps:
            if d is None:
                continue
            if d.eng == eng and eng in ("pe", "sp"):
                cov[eng] = max(cov.get(eng, -1), d.val)
                continue
            if not self.NO_DEDUPE and cov.get(d.eng, -1) >= d.val:
                continue
            if seen.get(d.eng, -1) >= d.val:
                continue
            seen[d.eng] = d.val
            cov[d.eng] = max(cov.get(d.eng, -1), d.val)
        for k, v in seen.items():
            out.append((k, v))
        return out

    def step(self, eng, deps, emit):
        waits = self._waits(eng, deps)
        self.items.append((eng, waits, emit, False))
        self.counts[eng] += 1
        if eng in ("pe", "sp"):
            self.covered[eng][eng] = self.counts[eng]
        return Step(eng, self.counts[eng])

    def dma(self, eng, deps, pairs):
        waits = self._waits(eng, deps)
        self.items.append((eng, waits, list(pairs), True))
        self.counts[eng] += 16 * len(pairs)
        if eng in ("pe", "sp"):
            self.covered[eng][eng] = self.counts[eng]
        return Step(eng, self.counts[eng])

    def replay(self, block):
        sched = self

        def body(eng_key):
            def _(e):
                for (eng, waits, emit, is_dma) in sched.items:
                    if eng != eng_key:
                        continue
                    for (we, wv) in waits:
                        e.wait_ge(sched.sems[we], wv)
                    if is_dma:
                        for (o, i, kw) in emit:
                            e.dma_start(out=o, in_=i, **kw).then_inc(
                                sched.sems[eng], 16)
                    else:
                        last = emit(e)
                        last.then_inc(sched.sems[eng], 1)
            return _
        block.tensor(body("pe"))
        block.vector(body("dve"))
        block.scalar(body("act"))
        block.gpsimd(body("pool"))
        block.sync(body("sp"))


DEBUG_DUMPS = {}


def build_program():
    nc = bass.Bass("TRN2", target_bir_lowering=False)

    def din(name, shape, dt):
        return nc.dram_tensor(name, shape, dt, kind="ExternalInput").ap()

    xall = din("xall", [D, S], F32)
    wq_t = din("wq_t", [32, 128, D], F32)
    wk_t = din("wk_t", [8, 128, D], F32)
    wv_t = din("wv_t", [8, 128, D], F32)
    wo_t = din("wo_t", [32, 128, D], F32)
    g8 = din("g8", [NB_N, 64, D], U8)
    u8w = din("u8w", [NB_N, 64, D], U8)
    d8 = din("d8", [NB_D, 64, DFF], U8)
    sg_d = din("sg", [128, NB_N * 32], F32)
    su_d = din("su", [128, NB_N * 32], F32)
    sd_d = din("sd", [128, NB_D * 86], F32)
    zg8_d = din("zg8", [128, NB_N * 32], U8)
    zu8_d = din("zu8", [128, NB_N * 32], U8)
    zd8_d = din("zd8", [128, NB_D * 86], U8)
    maskX_d = din("maskX", [4, 128, 1024], BF16)
    maskY_d = din("maskY", [4, 128, 2048], BF16)
    idt_d = din("idt", [128, 128], F32)
    idtb_d = din("idtb", [128, 128], BF16)
    out_d = nc.dram_tensor("out", [D, 1024], F32, kind="ExternalOutput").ap()

    stack = ExitStack()
    with stack:
        AR_BYTES = 212800
        arena_t = stack.enter_context(nc.sbuf_tensor("arena", [128, AR_BYTES], U8))
        arena = arena_t[:]
        _off = [0]

        def alloc(nbytes, align=64):
            o = (_off[0] + align - 1) // align * align
            _off[0] = o + nbytes
            assert _off[0] <= AR_BYTES, f"arena overflow {_off[0]}"
            return o

        def vw(off, dt, rows, cols):
            esz = {F32: 4, BF16: 2, U8: 1, U16: 2}[dt]
            ap = arena[0:rows, off:off + cols * esz]
            return ap.bitcast(dt) if dt != U8 else ap

        o_idt = alloc(512); idt = vw(o_idt, F32, 128, 128)
        o_idtb = alloc(256); idtb = vw(o_idtb, BF16, 128, 128)
        o_ones = alloc(64); ones = vw(o_ones, F32, 128, 1)
        o_vec = alloc(2 * 4096)

        def vec(i):
            return vw(o_vec + i * 4096, F32, 1, 1024)
        o_stage = alloc(1024 * 4); stage = vw(o_stage, F32, 128, 1024)
        o_xin = alloc(2 * 1024 * 4)

        def xin(i):
            return vw(o_xin + i * 4096, F32, 128, 1024)
        o_mb = alloc(1024 * 4); Mb = vw(o_mb, F32, 128, 1024)
        o_rb = alloc(1024 * 4); Rb = vw(o_rb, F32, 128, 1024)
        o_mx = alloc(64); mxbuf = vw(o_mx, F32, 128, 4)
        o_ngm = alloc(64); negmax = vw(o_ngm, F32, 128, 1)
        o_zrow = alloc(64); zrow = vw(o_zrow, F32, 128, 1)
        o_rz = alloc(64); rz = vw(o_rz, F32, 128, 1)

        o_big = (_off[0] + 63) // 64 * 64

        oA = [o_big]

        def allocA(nbytes, align=64):
            o = (oA[0] + align - 1) // align * align
            oA[0] = o + nbytes
            assert oA[0] <= AR_BYTES, f"phase A overflow {oA[0]}"
            return o
        o_xn1 = allocA(32 * 1024)

        def xn1(i):
            return vw(o_xn1 + i * 1024, BF16, 128, 512)
        o_ao = o_xn1
        AOall = vw(o_ao, BF16, 128, 16384)

        def AO(h):
            return vw(o_ao + h * 1024, BF16, 128, 512)
        o_k = allocA(8 * 4096)

        def Kb(kvh):
            return vw(o_k + kvh * 4096, BF16, 128, 2048)
        o_vt = allocA(16 * 2048)
        VTall = vw(o_vt, BF16, 128, 16384)

        def VT(jt):
            return vw(o_vt + jt * 2048, BF16, 128, 1024)
        o_q = allocA(32 * 1024)

        def Qs(h):
            return vw(o_q + h * 1024, BF16, 128, 512)
        o_msk = allocA(4096)
        mask = vw(o_msk, BF16, 128, 2048)
        o_pb = allocA(4096)
        Pb = vw(o_pb, BF16, 128, 2048)
        o_pt = allocA(4096)
        PT = vw(o_pt, BF16, 128, 2048)
        o_ws = allocA(2 * 8192)

        def wsl(i):
            return vw(o_ws + i * 8192, BF16, 128, 4096)
        Vtmp = vw(o_pt, BF16, 128, 512)

        oB = [o_big]

        def allocB(nbytes, align=64):
            o = (oB[0] + align - 1) // align * align
            oB[0] = o + nbytes
            assert oB[0] <= AR_BYTES, f"phase B overflow {oB[0]}"
            return o
        o_xn2 = allocB(32 * 2048)

        def xn2(i):
            return vw(o_xn2 + i * 2048, BF16, 128, 1024)
        o_h = allocB(18 * 2048)

        def Hb(i):
            return vw(o_h + i * 2048, BF16, 128, 1024)
        o_wgmm = allocB(8192); Wgmm = vw(o_wgmm, BF16, 128, 4096)
        o_wumm = allocB(8192); Wumm = vw(o_wumm, BF16, 128, 4096)
        o_wdmm = allocB(4608); Wdmm = vw(o_wdmm, BF16, 128, 2304)
        o_gsilu = allocB(2048); gsilu = vw(o_gsilu, BF16, 128, 1024)
        o_w16 = allocB(8192); W16 = vw(o_w16, U16, 128, 4096)
        o_wf = allocB(8192); Wf = vw(o_wf, F32, 128, 2048)
        o_gsl = allocB(2 * 4096)

        def gslab(i):
            return vw(o_gsl + i * 4096, U8, 64, 4096)
        o_dsl = allocB(2 * 2304)

        def dslab(i):
            return vw(o_dsl + i * 2304, U8, 64, 2304)
        o_z16 = allocB(3 * 5504)
        z16g = vw(o_z16, U16, 128, 2752)
        z16u = vw(o_z16 + 5504, U16, 128, 2752)
        z16d = vw(o_z16 + 11008, U16, 128, 2752)
        o_zf = allocB(128 * 4)

        def zf_t(n):
            return vw(o_zf, F32, 128, n)
        o_sf = allocB(128 * 4)

        def sf_t(n):
            return vw(o_sf, F32, 128, n)
        o_mb2 = allocB(1024 * 4); Mb2 = vw(o_mb2, F32, 128, 1024)
        o_rb2 = allocB(1024 * 4); Rb2 = vw(o_rb2, F32, 128, 1024)
        zb8 = vw(o_mb2, U8, 128, 2752)

        psf = [stack.enter_context(nc.psum_tensor(f"ps{i}", [128, 512], F32))
               for i in range(6)]
        psb = [stack.enter_context(nc.psum_tensor(f"pb{i}", [128, 1024], BF16))
               for i in range(2)]

        def bank_f(i):
            return psf[i][:] if i < 6 else psb[i - 6][:].bitcast(F32)

        def bank_b(i):
            return psb[i - 6][:]

        sch = Sched(nc, stack)
        R = {}

        def res(name):
            if name not in R:
                R[name] = Res()
            return R[name]

        def rstep(eng, reads, writes, emit, extra=()):
            deps = list(extra)
            for n in reads:
                deps += res(n).rd()
            for n in writes:
                deps += res(n).wd()
            s = sch.step(eng, deps, emit)
            for n in reads:
                res(n).read(s)
            for n in writes:
                res(n).wrote(s)
            return s

        def rdma(q, reads, writes, pairs, extra=()):
            deps = list(extra)
            for n in reads:
                deps += res(n).rd()
            for n in writes:
                deps += res(n).wd()
            s = sch.dma(q, deps, pairs)
            for n in reads:
                res(n).read(s)
            for n in writes:
                res(n).wrote(s)
            return s

        XN1 = [f"xn1_{i}" for i in range(NB_D)]
        XN2 = [f"xn2_{i}" for i in range(NB_D)]

        def dump(name, aps, reads):
            if name not in DEBUG_DUMPS:
                return
            pairs = []
            for i, ap in enumerate(aps):
                dt = ap.dtype
                dd = nc.dram_tensor(f"dump_{name}_{i}", list(ap.shape), dt,
                                    kind="ExternalOutput").ap()
                pairs.append((dd, ap, {}))
            allb = [Step(e, sch.counts[e]) for e in ENGS]
            rdma("sp", reads, [], pairs, extra=allb)

        rdma("sp", [], ["idt", "idtb"], [(idt, idt_d, {}), (idtb, idtb_d, {})])
        rstep("dve", [], ["ones"], lambda e: e.memset(ones, 1.0))

        def ln_stats(src_ap_fn, n_dt, extra_fn=None):
            for dti in range(n_dt):
                xi = xin(dti % 2)
                ex = extra_fn(dti) if extra_fn else []
                rdma("sp", [], [f"xin{dti%2}"], [(xi, src_ap_fn(dti), {})],
                     extra=ex)
                rstep("act", [f"xin{dti%2}"], ["stage"],
                      lambda e, xi=xi: e.activation(stage, xi, AF.Square))

                def emit_stats(e, xi=xi, dti=dti, n_dt=n_dt):
                    first = (dti == 0)
                    last = (dti == n_dt - 1)
                    e.matmul(bank_f(2)[0:1, :], ones, xi[:, 0:512],
                             start=first, stop=last)
                    e.matmul(bank_f(3)[0:1, :], ones, xi[:, 512:1024],
                             start=first, stop=last)
                    e.matmul(bank_f(4)[0:1, :], ones, stage[:, 0:512],
                             start=first, stop=last)
                    return e.matmul(bank_f(5)[0:1, :], ones, stage[:, 512:1024],
                                    start=first, stop=last)
                rstep("pe", [f"xin{dti%2}", "stage"],
                      ["bk2", "bk3", "bk4", "bk5"], emit_stats)

            def emit_cp(e):
                e.tensor_copy(vec(0)[0:1, 0:512], bank_f(2)[0:1, :])
                e.tensor_copy(vec(0)[0:1, 512:1024], bank_f(3)[0:1, :])
                e.tensor_copy(vec(1)[0:1, 0:512], bank_f(4)[0:1, :])
                return e.tensor_copy(vec(1)[0:1, 512:1024], bank_f(5)[0:1, :])
            rstep("dve", ["bk2", "bk3", "bk4", "bk5"], ["vec0", "vec1"], emit_cp)
            rstep("dve", [], ["vec0"],
                  lambda e: e.tensor_scalar(vec(0), vec(0), 1.0 / D, None,
                                            ALU.mult))
            rstep("dve", [], ["vec1"],
                  lambda e: e.tensor_scalar(vec(1), vec(1), 1.0 / D, None,
                                            ALU.mult))
            rstep("act", ["vec0"], ["stage"],
                  lambda e: e.activation(stage[0:1, :], vec(0), AF.Square))
            rstep("dve", ["stage"], ["vec1"],
                  lambda e: e.scalar_tensor_tensor(vec(1), vec(1), 1e-5,
                                                   stage[0:1, :],
                                                   ALU.add, ALU.subtract))
            rstep("act", [], ["vec1"],
                  lambda e: e.activation(vec(1), vec(1), AF.Ln))
            rstep("act", [], ["vec1"],
                  lambda e: e.activation(vec(1), vec(1), AF.Exp, scale=-0.5))

        def ln_bcast(Mt, Rt, mname, rname):
            def emit_bc(e):
                onesrow = ones[0:1, 0:1].broadcast_to([1, 128])
                e.matmul(bank_f(0), onesrow, vec(0)[0:1, 0:512],
                         start=True, stop=True)
                e.matmul(bank_f(1), onesrow, vec(0)[0:1, 512:1024],
                         start=True, stop=True)
                e.matmul(bank_f(2), onesrow, vec(1)[0:1, 0:512],
                         start=True, stop=True)
                return e.matmul(bank_f(3), onesrow, vec(1)[0:1, 512:1024],
                                start=True, stop=True)
            rstep("pe", ["vec0", "vec1"], ["bk0", "bk1", "bk2", "bk3"], emit_bc)

            def emit_mbrb(e):
                e.tensor_copy(Mt[:, 0:512], bank_f(0))
                e.tensor_copy(Mt[:, 512:1024], bank_f(1))
                e.tensor_copy(Rt[:, 0:512], bank_f(2))
                return e.tensor_copy(Rt[:, 512:1024], bank_f(3))
            rstep("dve", ["bk0", "bk1", "bk2", "bk3"], [mname, rname], emit_mbrb)

        x1_steps = {}
        for half in range(2):
            hc = half * 1024
            E = 2048 if half else 1024
            nch = E // 512
            njt = E // 128
            mk_d = maskY_d if half else maskX_d

            ln_stats(lambda dti: xall[dti*128:(dti+1)*128, hc:hc+1024], NB_D)
            ln_bcast(Mb, Rb, "Mb", "Rb")

            for qt in range(2):
                qc = hc + qt * 512
                for dti in range(NB_D):
                    xi = xin(dti % 2)
                    rdma("sp", [], [f"xin{dti%2}"],
                         [(xi[:, 0:512],
                           xall[dti*128:(dti+1)*128, qc:qc+512], {})])
                    rstep("pool", [f"xin{dti%2}", "Mb"], ["stage"],
                          lambda e, xi=xi, qt=qt: e.tensor_sub(
                              stage[:, 0:512], xi[:, 0:512],
                              Mb[:, qt*512:qt*512+512]))
                    alias_w = ([f"AO{i}" for i in range(NHEAD)]
                               if (half == 1 and qt == 0) else [])
                    rstep("pool", ["stage", "Rb"], [XN1[dti]] + alias_w,
                          lambda e, dti=dti, qt=qt: e.tensor_mul(
                              xn1(dti), stage[:, 0:512],
                              Rb[:, qt*512:qt*512+512]))

                for ct in range(16):
                    is_v = ct >= 8
                    kvh = ct % 8
                    wb = wsl(ct % 2)
                    rdma("pool", [], [f"wsl{ct%2}"],
                         [(wb, (wv_t if is_v else wk_t)[kvh], {})])
                    pb = ct % 2

                    def emit_kv(e, wb=wb, pb=pb):
                        for dti in range(NB_D):
                            last = e.matmul(
                                bank_f(pb), wb[:, dti*128:(dti+1)*128],
                                xn1(dti)[:, 0:512],
                                start=(dti == 0), stop=(dti == NB_D - 1))
                        return last
                    rstep("pe", [f"wsl{ct%2}"] + XN1, [f"bk{pb}"], emit_kv)
                    if not is_v:
                        rstep("act", [f"bk{pb}"], [f"K{kvh}"],
                              lambda e, kvh=kvh, pb=pb, qc=qc:
                              e.copy(Kb(kvh)[:, qc:qc+512], bank_f(pb)))
                    else:
                        rstep("act", [f"bk{pb}"], ["PT"],
                              lambda e, pb=pb: e.copy(Vtmp, bank_f(pb)))
                        tb = 6 + ct % 2

                        def emit_vtr(e, tb=tb):
                            for i in range(4):
                                last = e.transpose(
                                    bank_b(tb)[:, i*128:(i+1)*128],
                                    Vtmp[:, i*128:(i+1)*128], idtb)
                            return last
                        rstep("pe", ["PT", "idtb"], [f"bk{tb}"], emit_vtr)
                        jt0 = qc // 128

                        def emit_vev(e, tb=tb, jt0=jt0, kvh=kvh):
                            dst = VTall.rearrange("p (j c) -> p j c", j=16)[
                                :, jt0:jt0+4, kvh*128:(kvh+1)*128]
                            src = bank_b(tb)[:, 0:512].rearrange(
                                "p (j c) -> p j c", j=4)
                            return e.tensor_copy(dst, src)
                        rstep("dve", [f"bk{tb}"], ["VT"], emit_vev)

                dump(f"xn1_{half}_{qt}", [xn1(i) for i in range(NB_D)], XN1)
                dump(f"K_{half}_{qt}", [Kb(k) for k in range(8)],
                     [f"K{k}" for k in range(8)])
                dump(f"VT_{half}_{qt}", [VTall], ["VT"])
                if qt == 0:
                    for h in range(NHEAD):
                        wb = wsl(h % 2)
                        rdma("pool", [], [f"wsl{h%2}"], [(wb, wq_t[h], {})])
                        pb = h % 2

                        def emit_q(e, wb=wb, pb=pb):
                            for dti in range(NB_D):
                                last = e.matmul(
                                    bank_f(pb), wb[:, dti*128:(dti+1)*128],
                                    xn1(dti)[:, 0:512],
                                    start=(dti == 0), stop=(dti == NB_D - 1))
                            return last
                        rstep("pe", [f"wsl{h%2}"] + XN1, [f"bk{pb}"], emit_q)
                        q_done = rstep("act", [f"bk{pb}"], [f"Q{h}"],
                                       lambda e, h=h, pb=pb: e.mul(
                                           Qs(h), bank_f(pb), SCALE))

            dump(f"Q_{half}", [Qs(h) for h in range(NHEAD)],
                 [f"Q{h}" for h in range(NHEAD)])
            for u in range(4):
                rdma("sp", [], ["mask"], [(mask[:, 0:E], mk_d[u], {})],
                     extra=[q_done])
                dump(f"mask_{half}_{u}", [mask[:, 0:E]], ["mask"])
                if f"maskalt_{half}_{u}" in DEBUG_DUMPS:
                    rdma("sp", [], ["Pb"], [(Pb[:, 0:E], mk_d[u], {})])
                    dump(f"maskalt_{half}_{u}", [Pb[:, 0:E]], ["Pb"])
                for h in range(NHEAD):
                    kvh = h // 4

                    def emit_sc(e, h=h, u=u, kvh=kvh, nch=nch):
                        for c in range(nch):
                            last = e.matmul(bank_f(2 + c),
                                            Qs(h)[:, u*128:(u+1)*128],
                                            Kb(kvh)[:, c*512:(c+1)*512],
                                            start=True, stop=True)
                        return last
                    rstep("pe", [f"Q{h}", f"K{kvh}"],
                          [f"bk{2+c}" for c in range(nch)], emit_sc)

                    def emit_mx(e, nch=nch):
                        for c in range(nch):
                            e.reduce_max(mxbuf[:, c:c+1], bank_f(2 + c), AX.X)
                        return e.reduce_max(negmax, mxbuf[:, 0:nch], AX.X,
                                            negate=True)
                    rstep("dve", [f"bk{2+c}" for c in range(nch)] + ["mx"],
                          ["mx", "negmax"], emit_mx)

                    def emit_exp(e, nch=nch):
                        for c in range(nch):
                            last = e.activation(Pb[:, c*512:(c+1)*512],
                                                bank_f(2 + c), AF.Exp,
                                                bias=negmax)
                        return last
                    rstep("act", ["negmax"] + [f"bk{2+c}" for c in range(nch)],
                          ["Pb"], emit_exp)
                    dump(f"Pexp_{half}_{u}_{h}", [Pb[:, 0:E]], ["Pb"])
                    dump(f"mx_{half}_{u}_{h}", [mxbuf, negmax], ["mx", "negmax"])
                    rstep("dve", ["Pb", "mask"], ["Pb", "zrow"],
                          lambda e, E=E: e.scalar_tensor_tensor(
                              Pb[:, 0:E], Pb[:, 0:E], 1.0, mask[:, 0:E],
                              ALU.mult, ALU.mult, accum_out=zrow))
                    rstep("act", ["zrow"], ["rz"],
                          lambda e: e.activation(rz, zrow, AF.Ln))
                    rstep("act", [], ["rz"],
                          lambda e: e.activation(rz, rz, AF.Exp, scale=-1.0))
                    dump(f"zr_{half}_{u}_{h}", [zrow, rz], ["zrow", "rz"])
                    rstep("dve", ["rz"], ["Pb"],
                          lambda e, E=E: e.tensor_scalar(Pb[:, 0:E], Pb[:, 0:E],
                                                         rz, None, ALU.mult))
                    dump(f"P_{half}_{u}_{h}", [Pb[:, 0:E]], ["Pb"])
                    for bi in range(njt // 8):
                        tb = 6 + bi % 2

                        def emit_ptr(e, bi=bi, tb=tb):
                            for i in range(8):
                                jt = bi * 8 + i
                                last = e.transpose(bank_b(tb)[:, i*128:(i+1)*128],
                                                   Pb[:, jt*128:(jt+1)*128],
                                                   idtb)
                            return last
                        rstep("pe", ["Pb", "idtb"], [f"bk{tb}"], emit_ptr)
                        rstep("dve", [f"bk{tb}"], ["PT"],
                              lambda e, bi=bi, tb=tb: e.tensor_copy(
                                  PT[:, bi*1024:(bi+1)*1024], bank_b(tb)))
                    pb = (h // 4) % 2

                    def emit_av(e, h=h, kvh=kvh, njt=njt, pb=pb):
                        sl = (h % 4) * 128
                        for jt in range(njt):
                            last = e.matmul(
                                bank_f(pb)[:, sl:sl+128],
                                VT(jt)[:, kvh*128:(kvh+1)*128],
                                PT[:, jt*128:(jt+1)*128],
                                start=(jt == 0), stop=(jt == njt - 1))
                        return last
                    rstep("pe", ["PT", "VT"], [f"bk{pb}"], emit_av)
                    if h % 4 == 3:
                        def emit_aoev(e, h=h, u=u, pb=pb):
                            dst = AOall.rearrange("p (hh q) -> p hh q", hh=32)[
                                :, h-3:h+1, u*128:(u+1)*128]
                            src = bank_f(pb).rearrange("p (hh q) -> p hh q",
                                                       hh=4)
                            return e.copy(dst, src)
                        rstep("act", [f"bk{pb}"],
                              [f"AO{h-3+i}" for i in range(4)], emit_aoev,
                              extra=[q_done])

            dump(f"masklate_{half}", [mask], ["mask"])
            dump(f"AO_{half}", [AOall], [f"AO{i}" for i in range(NHEAD)])
            for dct in range(NB_D):
                wb = wsl(dct % 2)
                rdma("pool", [], [f"wsl{dct%2}"], [(wb, wo_t[dct], {})])
                pb = dct % 2

                def emit_o(e, wb=wb, pb=pb):
                    for ht in range(NHEAD):
                        last = e.matmul(bank_f(pb), wb[:, ht*128:(ht+1)*128],
                                        AO(ht)[:, 0:512],
                                        start=(ht == 0), stop=(ht == NHEAD - 1))
                    return last
                rstep("pe", [f"wsl{dct%2}"] + [f"AO{i}" for i in range(NHEAD)],
                      [f"bk{pb}"], emit_o)
                xo = xin(dct % 2)
                rdma("sp", [], [f"xin{dct%2}"],
                     [(xo[:, 0:512], xall[dct*128:(dct+1)*128, hc:hc+512], {})])
                sl = (dct % 2) * 512
                rstep("dve", [f"bk{pb}", f"xin{dct%2}"], ["stage"],
                      lambda e, xo=xo, pb=pb, sl=sl: e.tensor_add(
                          stage[:, sl:sl+512], bank_f(pb), xo[:, 0:512]))
                st_w = rdma("sp", ["stage"], [f"out_{dct}"],
                            [(out_d[dct*128:(dct+1)*128, half*512:half*512+512],
                              stage[:, sl:sl+512], {})])
                x1_steps[(half, dct)] = st_w

        bar = sch.step("sp", [Step(e, sch.counts[e]) for e in ENGS],
                       lambda e: e.nop())

        dump("x1", [out_d], [f"out_{i}" for i in range(NB_D)])
        for (zdram, z16t, G) in ((zg8_d, z16g, 32), (zu8_d, z16u, 32),
                                 (zd8_d, z16d, 86)):
            zn = f"z16_{G}_{id(z16t)}"
            rdma("sp", [], ["Mb2"], [(zb8, zdram, {})], extra=[bar])

            def emit_zx(e, z16t=z16t, G=G):
                xl = zb8[0:64, :].bitcast(U16).rearrange(
                    "p (a g) -> p a g", g=G // 2)
                xh = zb8[64:128, :].bitcast(U16).rearrange(
                    "p (a g) -> p a g", g=G // 2)
                zl = z16t[0:64, :].rearrange("p (a g) -> p a g", g=G)
                zh = z16t[64:128, :].rearrange("p (a g) -> p a g", g=G)
                e.tensor_scalar(zl[:, :, 0:G//2], xl, 15, None, ALU.bitwise_and)
                e.tensor_scalar(zl[:, :, G//2:G], xl, 8, 15,
                                ALU.logical_shift_right, ALU.bitwise_and)
                e.tensor_scalar(zh[:, :, 0:G//2], xh, 4, 15,
                                ALU.logical_shift_right, ALU.bitwise_and)
                return e.tensor_scalar(zh[:, :, G//2:G], xh, 12, None,
                                       ALU.logical_shift_right)
            rstep("dve", ["Mb2"], [zn], emit_zx)
        ZG = f"z16_32_{id(z16g)}"
        ZU = f"z16_32_{id(z16u)}"
        ZD = f"z16_86_{id(z16d)}"

        ln_stats(lambda dti: out_d[dti*128:(dti+1)*128, :], NB_D,
                 extra_fn=lambda dti: [bar, x1_steps[(0, dti)],
                                       x1_steps[(1, dti)],
                                       res(f"out_{dti}").w])
        ln_bcast(Mb2, Rb2, "Mb2", "Rb2")
        for dti in range(NB_D):
            xi = xin(dti % 2)
            rdma("sp", [f"out_{dti}"], [f"xin{dti%2}"],
                 [(xi, out_d[dti*128:(dti+1)*128, :], {})])
            rstep("pool", [f"xin{dti%2}", "Mb2", "stage"], ["stage"],
                  lambda e, xi=xi: e.tensor_sub(stage, xi, Mb2))
            s = rstep("pool", ["stage", "Rb2"], [XN2[dti]],
                      lambda e, dti=dti: e.tensor_mul(xn2(dti), stage, Rb2))
        ln2_done = s

        dump("xn2", [xn2(i) for i in range(NB_D)], XN2)
        dump("z16", [z16g, z16u, z16d], [ZG, ZU, ZD])

        def dequant_gu(slab_dram, sdram, z16t, zres, a, which, Wmm):
            si = a % 2
            sl = gslab(si)
            rdma("sp", [], [f"gsl{si}", "sf"],
                 [(sl, slab_dram[a], {}),
                  (sf_t(32), sdram[:, a*32:(a+1)*32], {})])

            def emit_ex(e, sl=sl):
                x = sl[0:64, :].bitcast(U16)
                e.tensor_scalar(W16[0:64, 0:2048], x, 15, None, ALU.bitwise_and)
                e.tensor_scalar(W16[64:128, 0:2048], x, 4, 15,
                                ALU.logical_shift_right, ALU.bitwise_and)
                e.tensor_scalar(W16[0:64, 2048:4096], x, 8, 15,
                                ALU.logical_shift_right, ALU.bitwise_and)
                return e.tensor_scalar(W16[64:128, 2048:4096], x, 12, None,
                                       ALU.logical_shift_right)
            rstep("dve", [f"gsl{si}"], ["W16"], emit_ex)
            dump(f"w16_{which}_{a}", [W16], ["W16"])
            rstep("dve", [zres], ["zf"],
                  lambda e, z16t=z16t, a=a: e.tensor_copy(
                      zf_t(32), z16t[:, a*32:(a+1)*32]))
            dump(f"zf_{which}_{a}", [zf_t(32), sf_t(32)], ["zf", "sf"])
            for hf in range(2):
                def emit_ts(e, hf=hf):
                    for i in range(16):
                        g = hf * 16 + i
                        col = int(GPOS_32[g])
                        last = e.tensor_scalar(
                            Wf[:, i*128:(i+1)*128], W16[:, g*128:(g+1)*128],
                            zf_t(32)[:, col:col+1], sf_t(32)[:, g:g+1],
                            ALU.subtract, ALU.mult)
                    return last
                rstep("dve" if which == "g" else "pool",
                      ["W16", "zf", "sf"], ["Wf"], emit_ts)
                dump(f"wf_{which}_{a}_{hf}", [Wf], ["Wf"])
                for bi in range(4):
                    tb = 6 + bi % 2

                    def emit_wtr(e, bi=bi, tb=tb):
                        for i in range(4):
                            last = e.transpose(bank_f(tb)[:, i*128:(i+1)*128],
                                               Wf[:, (bi*4+i)*128:(bi*4+i+1)*128],
                                               idt)
                        return last
                    rstep("pe", ["Wf", "idt"], [f"bk{tb}"], emit_wtr)
                    rstep("act", [f"bk{tb}"], [f"Wmm_{which}"],
                          lambda e, bi=bi, tb=tb, hf=hf, Wmm=Wmm: e.copy(
                              Wmm[:, hf*2048+bi*512:hf*2048+bi*512+512],
                              bank_f(tb)))

        for r, RLEN in enumerate(RUNS):
            L = RLEN * 128
            for q in range(RLEN):
                a = RUN_STARTS[r] + q
                dequant_gu(g8, sg_d, z16g, ZG, a, "g", Wgmm)
                dump(f"wg_{a}", [Wgmm], ["Wmm_g"])
                for tc in range(2):
                    def emit_g(e, tc=tc):
                        for g in range(32):
                            last = e.matmul(bank_f(tc),
                                            Wgmm[:, g*128:(g+1)*128],
                                            xn2(g)[:, tc*512:(tc+1)*512],
                                            start=(g == 0), stop=(g == 31))
                        return last
                    rstep("pe", ["Wmm_g"] + XN2, [f"bk{tc}"], emit_g,
                          extra=[ln2_done])
                    rstep("act", [f"bk{tc}"], ["gsilu"],
                          lambda e, tc=tc: e.activation(
                              gsilu[:, tc*512:(tc+1)*512], bank_f(tc), AF.Silu))
                dump(f"gs_{a}", [gsilu], ["gsilu"])
                dequant_gu(u8w, su_d, z16u, ZU, a, "u", Wumm)
                for tc in range(2):
                    def emit_u(e, tc=tc):
                        for g in range(32):
                            last = e.matmul(bank_f(2+tc),
                                            Wumm[:, g*128:(g+1)*128],
                                            xn2(g)[:, tc*512:(tc+1)*512],
                                            start=(g == 0), stop=(g == 31))
                        return last
                    rstep("pe", ["Wmm_u"] + XN2, [f"bk{2+tc}"], emit_u)
                    rstep("dve", [f"bk{2+tc}", "gsilu"], [f"H{q}"],
                          lambda e, tc=tc, q=q: e.tensor_tensor(
                              Hb(q)[:, tc*512:(tc+1)*512], bank_f(2+tc),
                              gsilu[:, tc*512:(tc+1)*512], ALU.mult))
            for q in range(RLEN):
                dump(f"h_{r}_{q}", [Hb(q)], [f"H{q}"])
            for t in range(NB_D):
                si = t % 2
                sl = dslab(si)
                rdma("sp", [], [f"dsl{si}", "sf"],
                     [(sl[0:64, 0:L], d8[t][:, RUN_STARTS[r]*128:
                                            RUN_STARTS[r]*128+L], {}),
                      (sf_t(RLEN), sd_d[:, t*86+RUN_STARTS[r]:
                                        t*86+RUN_STARTS[r]+RLEN], {})])

                def emit_exd(e, sl=sl, L=L):
                    x = sl[0:64, 0:L].bitcast(U16)
                    e.tensor_scalar(W16[0:64, 0:L//2], x, 15, None,
                                    ALU.bitwise_and)
                    e.tensor_scalar(W16[64:128, 0:L//2], x, 4, 15,
                                    ALU.logical_shift_right, ALU.bitwise_and)
                    e.tensor_scalar(W16[0:64, L//2:L], x, 8, 15,
                                    ALU.logical_shift_right, ALU.bitwise_and)
                    return e.tensor_scalar(W16[64:128, L//2:L], x, 12, None,
                                           ALU.logical_shift_right)
                rstep("dve", [f"dsl{si}"], ["W16"], emit_exd)
                s0 = RUN_STARTS[r]

                def emit_zfd(e, t=t, s0=s0, RLEN=RLEN):
                    zfv = zf_t(RLEN).rearrange("p (a two) -> p a two", two=2)
                    c_ev = t * 86 + s0 // 2
                    c_od = t * 86 + 43 + s0 // 2
                    e.tensor_copy(zfv[:, :, 0:1],
                                  z16d[:, c_ev:c_ev+RLEN//2].unsqueeze(2))
                    return e.tensor_copy(zfv[:, :, 1:2],
                                         z16d[:, c_od:c_od+RLEN//2].unsqueeze(2))
                rstep("dve", [ZD], ["zf"], emit_zfd)
                for hf in range(2):
                    nblk = RLEN // 2

                    def emit_tsd(e, hf=hf, nblk=nblk, RLEN=RLEN):
                        for i in range(nblk):
                            q2 = hf * nblk + i
                            last = e.tensor_scalar(
                                Wf[:, i*128:(i+1)*128], W16[:, q2*128:(q2+1)*128],
                                zf_t(RLEN)[:, q2:q2+1], sf_t(RLEN)[:, q2:q2+1],
                                ALU.subtract, ALU.mult)
                        return last
                    rstep("pool" if t % 2 else "dve", ["W16", "zf", "sf"],
                          ["Wf"], emit_tsd)
                    nb4 = (nblk + 3) // 4
                    for bi in range(nb4):
                        tb = 6 + bi % 2
                        lo = bi * 4
                        hi2 = min(lo + 4, nblk)

                        def emit_wtrd(e, lo=lo, hi2=hi2, tb=tb):
                            for i in range(hi2 - lo):
                                last = e.transpose(
                                    bank_f(tb)[:, i*128:(i+1)*128],
                                    Wf[:, (lo+i)*128:(lo+i+1)*128], idt)
                            return last
                        rstep("pe", ["Wf", "idt"], [f"bk{tb}"], emit_wtrd)
                        rstep("act", [f"bk{tb}"], ["Wmm_d"],
                              lambda e, lo=lo, hi2=hi2, tb=tb, hf=hf, nblk=nblk:
                              e.copy(Wdmm[:, (hf*nblk+lo)*128:
                                          (hf*nblk+lo)*128+(hi2-lo)*128],
                                     bank_f(tb)[:, 0:(hi2-lo)*128]))
                dump(f"wd_{r}_{t}", [Wdmm[:, 0:L]], ["Wmm_d"])
                dump(f"wf_d_{r}_{t}", [Wf], ["Wf"])
                for tc in range(2):
                    def emit_d(e, tc=tc, RLEN=RLEN):
                        for q2 in range(RLEN):
                            last = e.matmul(bank_f(4+tc),
                                            Wdmm[:, q2*128:(q2+1)*128],
                                            Hb(q2)[:, tc*512:(tc+1)*512],
                                            start=(q2 == 0),
                                            stop=(q2 == RLEN - 1))
                        return last
                    rstep("pe", ["Wmm_d"] + [f"H{i}" for i in range(RLEN)],
                          [f"bk{4+tc}"], emit_d)
                    rstep("dve", [f"bk{4+tc}"], ["stage"],
                          lambda e, tc=tc: e.tensor_copy(
                              stage[:, tc*512:(tc+1)*512], bank_f(4+tc)))
                dump(f"dstage_{r}_{t}", [stage], ["stage"])
                rdma("pool", ["stage"], [f"out_{t}"],
                     [(out_d[t*128:(t+1)*128, :], stage,
                       {"accum_op": ALU.add})], extra=[ln2_done])
            dump(f"outmid_{r}", [out_d], [f"out_{i}" for i in range(NB_D)])

        sch.step("sp", [Step("pool", sch.counts["pool"]),
                        Step("sp", sch.counts["sp"])], lambda e: e.nop())

        block = stack.enter_context(nc.Block())
        sch.replay(block)
    return nc


_NC_CACHE = []


def kernel(L0_attn_x, L0_attn_wq, L0_attn_wk, L0_attn_wv, L0_attn_wo,
           L0_ffn_gate_qweight, L0_ffn_gate_scales, L0_ffn_gate_qzeros,
           L0_ffn_up_qweight, L0_ffn_up_scales, L0_ffn_up_qzeros,
           L0_ffn_down_qweight, L0_ffn_down_scales, L0_ffn_down_qzeros):
    x = np.asarray(L0_attn_x, dtype=np.float32)
    wq_t = _tile_kpart(np.asarray(L0_attn_wq, dtype=np.float32))
    wk_t = _tile_kpart(np.asarray(L0_attn_wk, dtype=np.float32))
    wv_t = _tile_kpart(np.asarray(L0_attn_wv, dtype=np.float32))
    wo_t = _tile_wo(np.asarray(L0_attn_wo, dtype=np.float32))
    g8 = _gateup_slabs(np.asarray(L0_ffn_gate_qweight))
    u8w = _gateup_slabs(np.asarray(L0_ffn_up_qweight))
    d8 = _down_slabs(np.asarray(L0_ffn_down_qweight))
    sg = _gateup_scales(np.asarray(L0_ffn_gate_scales))
    su = _gateup_scales(np.asarray(L0_ffn_up_scales))
    sd = _down_scales(np.asarray(L0_ffn_down_scales))
    zg8 = _gateup_zero_bytes(np.asarray(L0_ffn_gate_qzeros))
    zu8 = _gateup_zero_bytes(np.asarray(L0_ffn_up_qzeros))
    zd8 = _down_zero_bytes(np.asarray(L0_ffn_down_qzeros))
    idt = np.eye(128, dtype=np.float32)
    idtb = np.eye(128).astype(ml_dtypes.bfloat16)

    xt = np.ascontiguousarray(x.transpose(0, 2, 1))[:, PI_D, :]

    in_maps = []
    core_meta = []
    for c in range(8):
        b, par = c // 2, c % 2
        if par == 0:
            ownX, restH0 = (0, 512), (512, 1024)
            ownY, restH1 = (1536, 2048), (1024, 1536)
        else:
            ownX, restH0 = (512, 1024), (0, 512)
            ownY, restH1 = (1024, 1536), (1536, 2048)
        colord = np.concatenate([np.arange(*ownX), np.arange(*restH0),
                                 np.arange(*ownY), np.arange(*restH1)])
        xc = np.ascontiguousarray(xt[b][:, colord])
        jabs = colord
        maskX = np.empty((4, 128, 1024), dtype=ml_dtypes.bfloat16)
        maskY = np.empty((4, 128, 2048), dtype=ml_dtypes.bfloat16)
        for u in range(4):
            qX = ownX[0] + 128 * u + np.arange(128)[:, None]
            maskX[u] = (jabs[None, :1024] <= qX).astype(ml_dtypes.bfloat16)
            qY = ownY[0] + 128 * u + np.arange(128)[:, None]
            maskY[u] = (jabs[None, :2048] <= qY).astype(ml_dtypes.bfloat16)
        in_maps.append(dict(
            xall=xc, wq_t=wq_t, wk_t=wk_t, wv_t=wv_t, wo_t=wo_t,
            g8=g8, u8w=u8w, d8=d8, sg=sg, su=su, sd=sd,
            zg8=zg8, zu8=zu8, zd8=zd8, maskX=maskX, maskY=maskY,
            idt=idt, idtb=idtb))
        core_meta.append((b, ownX, ownY))

    if not _NC_CACHE:
        _NC_CACHE.append(build_program())
    nc = _NC_CACHE[0]
    resobj = run_bass_kernel_spmd(nc, in_maps, core_ids=list(range(8)))

    out = np.empty((4, 2048, 4096), dtype=np.float32)
    for c in range(8):
        b, ownX, ownY = core_meta[c]
        o = np.asarray(resobj.results[c]["out"])
        out[b, ownX[0]:ownX[1], PI_D] = o[:, 0:512]
        out[b, ownY[0]:ownY[1], PI_D] = o[:, 512:1024]
    return out




# version 06
# speedup vs baseline: 1.2749x; 1.2749x over previous
import numpy as np
import ml_dtypes
from contextlib import ExitStack
import concourse.bass as bass
import concourse.mybir as mybir
from concourse.bass_utils import run_bass_kernel_spmd

F32 = mybir.dt.float32
BF16 = mybir.dt.bfloat16
U8 = mybir.dt.uint8
U16 = mybir.dt.uint16
ALU = mybir.AluOpType
AF = mybir.ActivationFunctionType
AX = mybir.AxisListType

D = 4096
DFF = 11008
S = 2048
NB_D = 32
NB_N = 86
RUNS = [16, 16, 16, 16, 16, 6]
RUN_STARTS = [0, 16, 32, 48, 64, 80]
SCALE = 1.0 / (128.0 ** 0.5)
NHEAD = 32
NKV = 8

_LUTLO = np.array([0, 4, 1, 5])
_LUTHI = np.array([2, 6, 3, 7])


def _canonical_perm(N):
    pi = np.empty(N, dtype=np.int64)
    a = np.arange(N // 128)
    for p in range(128):
        if p < 64:
            m = 64 * a + p
            pi[a * 128 + p] = 8 * (m // 4) + _LUTLO[p % 4]
        else:
            m = 64 * a + (p - 64)
            pi[a * 128 + p] = 8 * (m // 4) + _LUTHI[(p - 64) % 4]
    return pi


PI_D = _canonical_perm(D)
PI_N = _canonical_perm(DFF)
GPOS_32 = np.argsort(np.concatenate([np.arange(0, 32, 2), np.arange(1, 32, 2)]))
GPOS_86 = np.argsort(np.concatenate([np.arange(0, 86, 2), np.arange(1, 86, 2)]))


def _interleave_idx(L):
    ki = np.empty(L, dtype=np.int64)
    ki[0::2] = np.arange(L // 2)
    ki[1::2] = L // 2 + np.arange(L // 2)
    return ki


def _gateup_slabs(qweight):
    qb = np.ascontiguousarray(qweight).view(np.uint8).reshape(D, DFF // 2)
    qb = qb[PI_D, :][_interleave_idx(D), :]
    return np.ascontiguousarray(qb.reshape(D, NB_N, 64).transpose(1, 2, 0))


def _down_slabs(qweight):
    db = np.ascontiguousarray(qweight).view(np.uint8).reshape(DFF, D // 2)
    db = db[PI_N, :]
    out = np.empty((NB_D, 64, DFF), dtype=np.uint8)
    for r, R in enumerate(RUNS):
        lo = RUN_STARTS[r] * 128
        L = R * 128
        sub = db[lo + _interleave_idx(L), :]
        out[:, :, lo:lo + L] = sub.reshape(L, NB_D, 64).transpose(1, 2, 0)
    return np.ascontiguousarray(out)


def _gateup_scales(scales):
    sc = scales[:, PI_N]
    t = sc.T.reshape(NB_N, 128, 32).transpose(1, 0, 2).reshape(128, NB_N * 32)
    return np.ascontiguousarray(t.astype(np.float32))


def _down_scales(scales):
    sc = scales[:, PI_D]
    t = sc.T.reshape(NB_D, 128, 86).transpose(1, 0, 2)
    out = np.empty((128, NB_D * 86), dtype=np.float32)
    off = 0
    for r, R in enumerate(RUNS):
        s0 = RUN_STARTS[r]
        blk = t[:, :, s0:s0 + R]
        out[:, off:off + 32 * R] = blk.reshape(128, 32 * R)
        off += 32 * R
    return np.ascontiguousarray(out)


def _gateup_zero_bytes(qzeros):
    zb = np.ascontiguousarray(qzeros).view(np.uint8).reshape(32, DFF // 2).T
    h = np.empty((128, NB_N * 32), dtype=np.uint8)
    for a in range(NB_N):
        blk = zb[64 * a:64 * a + 64, :]
        h[0:64, a * 32:(a + 1) * 32] = blk
        h[64:128, a * 32:(a + 1) * 32] = blk
    return h


def _down_zero_bytes(qzeros):
    zb = np.ascontiguousarray(qzeros).view(np.uint8).reshape(86, D // 2).T
    h = np.empty((128, NB_D * 86), dtype=np.uint8)
    for t in range(NB_D):
        blk = zb[64 * t:64 * t + 64, :]
        h[0:64, t * 86:(t + 1) * 86] = blk
        h[64:128, t * 86:(t + 1) * 86] = blk
    return h


def _tile_kpart(w):
    wc = w[PI_D, :]
    C = w.shape[1]
    t = wc.reshape(NB_D, 128, C // 128, 128).transpose(2, 1, 0, 3)
    return np.ascontiguousarray(t.reshape(C // 128, 128, NB_D * 128)
                                .astype(np.float32))


def _tile_wo(wo):
    wc = wo[:, PI_D]
    t = wc.reshape(32, 128, NB_D, 128).transpose(2, 1, 0, 3)
    return np.ascontiguousarray(t.reshape(NB_D, 128, 32 * 128).astype(np.float32))


ENGS = ("pe", "dve", "act", "pool", "sp")


class Step:
    __slots__ = ("eng", "val")
    def __init__(self, eng, val):
        self.eng = eng
        self.val = val


class Res:
    __slots__ = ("w", "rs")
    def __init__(self):
        self.w = None
        self.rs = []
    def rd(self):
        return [self.w] if self.w is not None else []
    def wd(self):
        d = list(self.rs)
        if self.w is not None:
            d.append(self.w)
        return d
    def read(self, s):
        self.rs.append(s)
    def wrote(self, s):
        self.w = s
        self.rs = []


class Sched:
    def __init__(self, nc, stack):
        self.nc = nc
        self.sems = {e: stack.enter_context(nc.semaphore(f"sem_{e}"))
                     for e in ENGS}
        self.counts = {e: 0 for e in ENGS}
        self.items = []
        self.covered = {e: {} for e in ENGS}

    NO_DEDUPE = True

    def _waits(self, eng, deps):
        cov = self.covered[eng]
        out = []
        seen = {}
        for d in deps:
            if d is None:
                continue
            if d.eng == eng and eng in ("pe", "sp"):
                cov[eng] = max(cov.get(eng, -1), d.val)
                continue
            if not self.NO_DEDUPE and cov.get(d.eng, -1) >= d.val:
                continue
            if seen.get(d.eng, -1) >= d.val:
                continue
            seen[d.eng] = d.val
            cov[d.eng] = max(cov.get(d.eng, -1), d.val)
        for k, v in seen.items():
            out.append((k, v))
        return out

    def step(self, eng, deps, emit):
        waits = self._waits(eng, deps)
        self.items.append((eng, waits, emit, False))
        self.counts[eng] += 1
        if eng in ("pe", "sp"):
            self.covered[eng][eng] = self.counts[eng]
        return Step(eng, self.counts[eng])

    def dma(self, eng, deps, pairs):
        waits = self._waits(eng, deps)
        self.items.append((eng, waits, list(pairs), True))
        self.counts[eng] += 16 * len(pairs)
        if eng in ("pe", "sp"):
            self.covered[eng][eng] = self.counts[eng]
        return Step(eng, self.counts[eng])

    def replay(self, block):
        sched = self

        def body(eng_key):
            def _(e):
                for (eng, waits, emit, is_dma) in sched.items:
                    if eng != eng_key:
                        continue
                    for (we, wv) in waits:
                        e.wait_ge(sched.sems[we], wv)
                    if is_dma:
                        for (o, i, kw) in emit:
                            e.dma_start(out=o, in_=i, **kw).then_inc(
                                sched.sems[eng], 16)
                    else:
                        last = emit(e)
                        last.then_inc(sched.sems[eng], 1)
            return _
        block.tensor(body("pe"))
        block.vector(body("dve"))
        block.scalar(body("act"))
        block.gpsimd(body("pool"))
        block.sync(body("sp"))


DEBUG_DUMPS = {}


def build_program():
    nc = bass.Bass("TRN2", target_bir_lowering=False)

    def din(name, shape, dt):
        return nc.dram_tensor(name, shape, dt, kind="ExternalInput").ap()

    xall = din("xall", [D, S], F32)
    wq_t = din("wq_t", [32, 128, D], F32)
    wk_t = din("wk_t", [8, 128, D], F32)
    wv_t = din("wv_t", [8, 128, D], F32)
    wo_t = din("wo_t", [32, 128, D], F32)
    g8 = din("g8", [NB_N, 64, D], U8)
    u8w = din("u8w", [NB_N, 64, D], U8)
    d8 = din("d8", [NB_D, 64, DFF], U8)
    sg_d = din("sg", [128, NB_N * 32], F32)
    su_d = din("su", [128, NB_N * 32], F32)
    sd_d = din("sd", [128, NB_D * 86], F32)
    zg8_d = din("zg8", [128, NB_N * 32], U8)
    zu8_d = din("zu8", [128, NB_N * 32], U8)
    zd8_d = din("zd8", [128, NB_D * 86], U8)
    maskX_d = din("maskX", [4, 128, 1024], BF16)
    maskY_d = din("maskY", [4, 128, 2048], BF16)
    idt_d = din("idt", [128, 128], F32)
    idtb_d = din("idtb", [128, 128], BF16)
    out_d = nc.dram_tensor("out", [D, 1024], F32, kind="ExternalOutput").ap()

    stack = ExitStack()
    with stack:
        AR_BYTES = 212800
        arena_t = stack.enter_context(nc.sbuf_tensor("arena", [128, AR_BYTES], U8))
        arena = arena_t[:]
        _off = [0]

        def alloc(nbytes, align=64):
            o = (_off[0] + align - 1) // align * align
            _off[0] = o + nbytes
            assert _off[0] <= AR_BYTES, f"arena overflow {_off[0]}"
            return o

        def vw(off, dt, rows, cols):
            esz = {F32: 4, BF16: 2, U8: 1, U16: 2}[dt]
            ap = arena[0:rows, off:off + cols * esz]
            return ap.bitcast(dt) if dt != U8 else ap

        o_idt = alloc(512); idt = vw(o_idt, F32, 128, 128)
        o_idtb = alloc(256); idtb = vw(o_idtb, BF16, 128, 128)
        o_ones = alloc(64); ones = vw(o_ones, F32, 128, 1)
        o_vec = alloc(2 * 4096)

        def vec(i):
            return vw(o_vec + i * 4096, F32, 1, 1024)
        o_stage = alloc(1024 * 4); stage = vw(o_stage, F32, 128, 1024)
        o_xin = alloc(2 * 1024 * 4)

        def xin(i):
            return vw(o_xin + i * 4096, F32, 128, 1024)
        o_mb = alloc(1024 * 4); Mb = vw(o_mb, F32, 128, 1024)
        o_rb = alloc(1024 * 4); Rb = vw(o_rb, F32, 128, 1024)
        o_mx = alloc(128)
        def mxbuf(i):
            return vw(o_mx + i * 64, F32, 128, 4)
        o_ngm = alloc(128)
        def negmax(i):
            return vw(o_ngm + i * 64, F32, 128, 1)
        o_zrow = alloc(128)
        def zrow(i):
            return vw(o_zrow + i * 64, F32, 128, 1)
        o_rz = alloc(128)
        def rz(i):
            return vw(o_rz + i * 64, F32, 128, 1)

        o_big = (_off[0] + 63) // 64 * 64

        oA = [o_big]

        def allocA(nbytes, align=64):
            o = (oA[0] + align - 1) // align * align
            oA[0] = o + nbytes
            assert oA[0] <= AR_BYTES, f"phase A overflow {oA[0]}"
            return o
        o_xn1 = allocA(32 * 1024)

        def xn1(i):
            return vw(o_xn1 + i * 1024, BF16, 128, 512)
        o_ao = o_xn1
        AOall = vw(o_ao, BF16, 128, 16384)

        def AO(h):
            return vw(o_ao + h * 1024, BF16, 128, 512)
        o_k = allocA(8 * 4096)

        def Kb(kvh):
            return vw(o_k + kvh * 4096, BF16, 128, 2048)
        o_vt = allocA(16 * 2048)
        VTall = vw(o_vt, BF16, 128, 16384)

        def VT(jt):
            return vw(o_vt + jt * 2048, BF16, 128, 1024)
        o_q = allocA(32 * 1024)

        def Qs(h):
            return vw(o_q + h * 1024, BF16, 128, 512)
        o_msk = allocA(16384)

        def maskv(u, E):
            return vw(o_msk + u * 2 * E, BF16, 128, E)
        o_pb = allocA(2 * 4096)

        def Pb(i):
            return vw(o_pb + i * 4096, BF16, 128, 2048)
        o_ws = allocA(2 * 8192)

        def wsl(i):
            return vw(o_ws + i * 8192, BF16, 128, 4096)
        o_pt = o_ws
        PTRES = ["wsl0", "wsl1"]

        def PTh(njt):
            return vw(o_pt, BF16, 128, 4 * njt * 128)
        Vtmp = vw(o_pb, BF16, 128, 512)

        oB = [o_big]

        def allocB(nbytes, align=64):
            o = (oB[0] + align - 1) // align * align
            oB[0] = o + nbytes
            assert oB[0] <= AR_BYTES, f"phase B overflow {oB[0]}"
            return o
        o_xn2 = allocB(32 * 2048)

        def xn2(i):
            return vw(o_xn2 + i * 2048, BF16, 128, 1024)
        o_h = allocB(16 * 2048)

        def Hb(i):
            return vw(o_h + i * 2048, BF16, 128, 1024)
        o_wgmm = allocB(8192); Wgmm = vw(o_wgmm, BF16, 128, 4096)
        o_wumm = allocB(8192); Wumm = vw(o_wumm, BF16, 128, 4096)
        o_wdmm = allocB(4096); Wdmm = vw(o_wdmm, BF16, 128, 2048)
        o_gsilu = allocB(2048); gsilu = vw(o_gsilu, BF16, 128, 1024)
        o_w16 = allocB(8192); W16 = vw(o_w16, U16, 128, 4096)
        o_wf = allocB(8192)
        o_gsl = allocB(2 * 4096)

        def gslab(i):
            return vw(o_gsl + i * 4096, U8, 64, 4096)
        o_dsl = allocB(2 * 2048)

        def dslab(i):
            return vw(o_dsl + i * 2048, U8, 64, 2048)
        o_z16 = allocB(3 * 5504)
        z16g = vw(o_z16, U16, 128, 2752)
        z16u = vw(o_z16 + 5504, U16, 128, 2752)
        z16d = vw(o_z16 + 11008, U16, 128, 2752)
        sgr = vw(o_mb, F32, 128, 512)
        sur = vw(o_mb + 2048, F32, 128, 512)
        zgr = vw(o_rb, F32, 128, 512)
        zur = vw(o_rb + 2048, F32, 128, 512)
        o_sdr = allocB(512 * 4); sdr = vw(o_sdr, F32, 128, 512)
        o_zdr = allocB(2 * 64)

        def zdr(i):
            return vw(o_zdr + i * 64, F32, 128, 16)
        o_mb2 = allocB(1024 * 4); Mb2 = vw(o_mb2, F32, 128, 1024)
        o_rb2 = allocB(1024 * 4); Rb2 = vw(o_rb2, F32, 128, 1024)
        zb8 = vw(o_mb2, U8, 128, 2752)
        def Wf(i):
            return vw(o_wf if i == 0 else o_mb2, F32, 128, 2048)
        def WFRES(i):
            return ["Wf0"] if i == 0 else ["Mb2", "Rb2"]

        psf = [stack.enter_context(nc.psum_tensor(f"ps{i}", [128, 512], F32))
               for i in range(6)]
        psb = [stack.enter_context(nc.psum_tensor(f"pb{i}", [128, 1024], BF16))
               for i in range(2)]

        def bank_f(i):
            return psf[i][:] if i < 6 else psb[i - 6][:].bitcast(F32)

        def bank_b(i):
            return psb[i - 6][:]

        sch = Sched(nc, stack)
        R = {}

        def res(name):
            if name not in R:
                R[name] = Res()
            return R[name]

        def rstep(eng, reads, writes, emit, extra=()):
            deps = list(extra)
            for n in reads:
                deps += res(n).rd()
            for n in writes:
                deps += res(n).wd()
            s = sch.step(eng, deps, emit)
            for n in reads:
                res(n).read(s)
            for n in writes:
                res(n).wrote(s)
            return s

        def rdma(q, reads, writes, pairs, extra=()):
            deps = list(extra)
            for n in reads:
                deps += res(n).rd()
            for n in writes:
                deps += res(n).wd()
            s = sch.dma(q, deps, pairs)
            for n in reads:
                res(n).read(s)
            for n in writes:
                res(n).wrote(s)
            return s

        XN1 = [f"xn1_{i}" for i in range(NB_D)]
        XN2 = [f"xn2_{i}" for i in range(NB_D)]

        def dump(name, aps, reads):
            if name not in DEBUG_DUMPS:
                return
            pairs = []
            for i, ap in enumerate(aps):
                dt = ap.dtype
                dd = nc.dram_tensor(f"dump_{name}_{i}", list(ap.shape), dt,
                                    kind="ExternalOutput").ap()
                pairs.append((dd, ap, {}))
            allb = [Step(e, sch.counts[e]) for e in ENGS]
            rdma("sp", reads, [], pairs, extra=allb)

        rdma("sp", [], ["idt", "idtb"], [(idt, idt_d, {}), (idtb, idtb_d, {})])
        rstep("dve", [], ["ones"], lambda e: e.memset(ones, 1.0))

        def ln_stats(src_ap_fn, n_dt, extra_fn=None):
            for dti in range(n_dt):
                xi = xin(dti % 2)
                ex = extra_fn(dti) if extra_fn else []
                rdma("sp", [], [f"xin{dti%2}"], [(xi, src_ap_fn(dti), {})],
                     extra=ex)
                rstep("act", [f"xin{dti%2}"], ["stage"],
                      lambda e, xi=xi: e.activation(stage, xi, AF.Square))

                def emit_stats(e, xi=xi, dti=dti, n_dt=n_dt):
                    first = (dti == 0)
                    last = (dti == n_dt - 1)
                    e.matmul(bank_f(2)[0:1, :], ones, xi[:, 0:512],
                             start=first, stop=last)
                    e.matmul(bank_f(3)[0:1, :], ones, xi[:, 512:1024],
                             start=first, stop=last)
                    e.matmul(bank_f(4)[0:1, :], ones, stage[:, 0:512],
                             start=first, stop=last)
                    return e.matmul(bank_f(5)[0:1, :], ones, stage[:, 512:1024],
                                    start=first, stop=last)
                rstep("pe", [f"xin{dti%2}", "stage"],
                      ["bk2", "bk3", "bk4", "bk5"], emit_stats)

            def emit_cp(e):
                e.tensor_copy(vec(0)[0:1, 0:512], bank_f(2)[0:1, :])
                e.tensor_copy(vec(0)[0:1, 512:1024], bank_f(3)[0:1, :])
                e.tensor_copy(vec(1)[0:1, 0:512], bank_f(4)[0:1, :])
                return e.tensor_copy(vec(1)[0:1, 512:1024], bank_f(5)[0:1, :])
            rstep("dve", ["bk2", "bk3", "bk4", "bk5"], ["vec0", "vec1"], emit_cp)
            rstep("dve", [], ["vec0"],
                  lambda e: e.tensor_scalar(vec(0), vec(0), 1.0 / D, None,
                                            ALU.mult))
            rstep("dve", [], ["vec1"],
                  lambda e: e.tensor_scalar(vec(1), vec(1), 1.0 / D, None,
                                            ALU.mult))
            rstep("act", ["vec0"], ["stage"],
                  lambda e: e.activation(stage[0:1, :], vec(0), AF.Square))
            rstep("dve", ["stage"], ["vec1"],
                  lambda e: e.scalar_tensor_tensor(vec(1), vec(1), 1e-5,
                                                   stage[0:1, :],
                                                   ALU.add, ALU.subtract))
            rstep("act", [], ["vec1"],
                  lambda e: e.activation(vec(1), vec(1), AF.Ln))
            rstep("act", [], ["vec1"],
                  lambda e: e.activation(vec(1), vec(1), AF.Exp, scale=-0.5))

        def ln_bcast(Mt, Rt, mname, rname):
            def emit_bc(e):
                onesrow = ones[0:1, 0:1].broadcast_to([1, 128])
                e.matmul(bank_f(0), onesrow, vec(0)[0:1, 0:512],
                         start=True, stop=True)
                e.matmul(bank_f(1), onesrow, vec(0)[0:1, 512:1024],
                         start=True, stop=True)
                e.matmul(bank_f(2), onesrow, vec(1)[0:1, 0:512],
                         start=True, stop=True)
                return e.matmul(bank_f(3), onesrow, vec(1)[0:1, 512:1024],
                                start=True, stop=True)
            rstep("pe", ["vec0", "vec1"], ["bk0", "bk1", "bk2", "bk3"], emit_bc)

            def emit_mbrb(e):
                e.tensor_copy(Mt[:, 0:512], bank_f(0))
                e.tensor_copy(Mt[:, 512:1024], bank_f(1))
                e.tensor_copy(Rt[:, 0:512], bank_f(2))
                return e.tensor_copy(Rt[:, 512:1024], bank_f(3))
            rstep("dve", ["bk0", "bk1", "bk2", "bk3"], [mname, rname], emit_mbrb)

        x1_steps = {}
        for half in range(2):
            hc = half * 1024
            E = 2048 if half else 1024
            nch = E // 512
            njt = E // 128
            mk_d = maskY_d if half else maskX_d

            ln_stats(lambda dti: xall[dti*128:(dti+1)*128, hc:hc+1024], NB_D)
            ln_bcast(Mb, Rb, "Mb", "Rb")

            for qt in range(2):
                qc = hc + qt * 512
                for dti in range(NB_D):
                    xi = xin(dti % 2)
                    rdma("sp", [], [f"xin{dti%2}"],
                         [(xi[:, 0:512],
                           xall[dti*128:(dti+1)*128, qc:qc+512], {})])
                    rstep("pool", [f"xin{dti%2}", "Mb"], ["stage"],
                          lambda e, xi=xi, qt=qt: e.tensor_sub(
                              stage[:, 0:512], xi[:, 0:512],
                              Mb[:, qt*512:qt*512+512]))
                    alias_w = ([f"AO{i}" for i in range(NHEAD)]
                               if (half == 1 and qt == 0) else [])
                    rstep("pool", ["stage", "Rb"], [XN1[dti]] + alias_w,
                          lambda e, dti=dti, qt=qt: e.tensor_mul(
                              xn1(dti), stage[:, 0:512],
                              Rb[:, qt*512:qt*512+512]))

                for ct in range(16):
                    is_v = ct >= 8
                    kvh = ct % 8
                    wb = wsl(ct % 2)
                    rdma("pool", [], [f"wsl{ct%2}"],
                         [(wb, (wv_t if is_v else wk_t)[kvh], {})])
                    pb = ct % 2

                    def emit_kv(e, wb=wb, pb=pb):
                        for dti in range(NB_D):
                            last = e.matmul(
                                bank_f(pb), wb[:, dti*128:(dti+1)*128],
                                xn1(dti)[:, 0:512],
                                start=(dti == 0), stop=(dti == NB_D - 1))
                        return last
                    rstep("pe", [f"wsl{ct%2}"] + XN1, [f"bk{pb}"], emit_kv)
                    if not is_v:
                        rstep("act", [f"bk{pb}"], [f"K{kvh}"],
                              lambda e, kvh=kvh, pb=pb, qc=qc:
                              e.copy(Kb(kvh)[:, qc:qc+512], bank_f(pb)))
                    else:
                        rstep("act", [f"bk{pb}"], ["PT"],
                              lambda e, pb=pb: e.copy(Vtmp, bank_f(pb)))
                        tb = 6 + ct % 2

                        def emit_vtr(e, tb=tb):
                            for i in range(4):
                                last = e.transpose(
                                    bank_b(tb)[:, i*128:(i+1)*128],
                                    Vtmp[:, i*128:(i+1)*128], idtb)
                            return last
                        rstep("pe", ["PT", "idtb"], [f"bk{tb}"], emit_vtr)
                        jt0 = qc // 128

                        def emit_vev(e, tb=tb, jt0=jt0, kvh=kvh):
                            dst = VTall.rearrange("p (j c) -> p j c", j=16)[
                                :, jt0:jt0+4, kvh*128:(kvh+1)*128]
                            src = bank_b(tb)[:, 0:512].rearrange(
                                "p (j c) -> p j c", j=4)
                            return e.tensor_copy(dst, src)
                        rstep("dve", [f"bk{tb}"], ["VT"], emit_vev)

                dump(f"xn1_{half}_{qt}", [xn1(i) for i in range(NB_D)], XN1)
                dump(f"K_{half}_{qt}", [Kb(k) for k in range(8)],
                     [f"K{k}" for k in range(8)])
                dump(f"VT_{half}_{qt}", [VTall], ["VT"])
                if qt == 0:
                    for h in range(NHEAD):
                        wb = wsl(h % 2)
                        rdma("pool", [], [f"wsl{h%2}"], [(wb, wq_t[h], {})])
                        pb = h % 2

                        def emit_q(e, wb=wb, pb=pb):
                            for dti in range(NB_D):
                                last = e.matmul(
                                    bank_f(pb), wb[:, dti*128:(dti+1)*128],
                                    xn1(dti)[:, 0:512],
                                    start=(dti == 0), stop=(dti == NB_D - 1))
                            return last
                        rstep("pe", [f"wsl{h%2}"] + XN1, [f"bk{pb}"], emit_q)
                        q_done = rstep("act", [f"bk{pb}"], [f"Q{h}"],
                                       lambda e, h=h, pb=pb: e.mul(
                                           Qs(h), bank_f(pb), SCALE))

            dump(f"Q_{half}", [Qs(h) for h in range(NHEAD)],
                 [f"Q{h}" for h in range(NHEAD)])
            rdma("sp", [], ["mask"],
                 [(maskv(u, E), mk_d[u], {}) for u in range(4)],
                 extra=[q_done])
            PT = PTh(njt)
            PTv = PT.rearrange("p (u c) -> p u c", u=4)
            for h in range(NHEAD):
                kvh = h // 4

                def softmax_u(h, u):
                    p = (h * 4 + u) % 2

                    def emit_sc(e, h=h, u=u, kvh=kvh, nch=nch):
                        for c in range(nch):
                            last = e.matmul(bank_f(2 + c),
                                            Qs(h)[:, u*128:(u+1)*128],
                                            Kb(kvh)[:, c*512:(c+1)*512],
                                            start=True, stop=True)
                        return last
                    rstep("pe", [f"Q{h}", f"K{kvh}"],
                          [f"bk{2+c}" for c in range(nch)], emit_sc)

                    def emit_mx(e, nch=nch, p=p):
                        for c in range(nch):
                            e.reduce_max(mxbuf(p)[:, c:c+1], bank_f(2 + c), AX.X)
                        return e.reduce_max(negmax(p), mxbuf(p)[:, 0:nch], AX.X,
                                            negate=True)
                    rstep("dve", [f"bk{2+c}" for c in range(nch)] + [f"mx{p}"],
                          [f"mx{p}", f"ngm{p}"], emit_mx)

                    def emit_exp(e, nch=nch, p=p, u=u):
                        for c in range(nch):
                            last = e.activation(Pb(u % 2)[:, c*512:(c+1)*512],
                                                bank_f(2 + c), AF.Exp,
                                                bias=negmax(p))
                        return last
                    rstep("act", [f"ngm{p}"] + [f"bk{2+c}" for c in range(nch)],
                          [f"Pb{u%2}"], emit_exp)
                    rstep("dve", [f"Pb{u%2}", "mask"], [f"Pb{u%2}", f"zr{p}"],
                          lambda e, E=E, u=u, p=p: e.scalar_tensor_tensor(
                              Pb(u % 2)[:, 0:E], Pb(u % 2)[:, 0:E], 1.0,
                              maskv(u, E), ALU.mult, ALU.mult,
                              accum_out=zrow(p)))
                    rstep("act", [f"zr{p}"], [f"rz{p}"],
                          lambda e, p=p: e.activation(rz(p), zrow(p), AF.Ln))
                    rstep("act", [], [f"rz{p}"],
                          lambda e, p=p: e.activation(rz(p), rz(p), AF.Exp,
                                                      scale=-1.0))
                    rstep("dve", [f"rz{p}"], [f"Pb{u%2}"],
                          lambda e, E=E, u=u, p=p: e.tensor_scalar(
                              Pb(u % 2)[:, 0:E], Pb(u % 2)[:, 0:E], rz(p), None,
                              ALU.mult))

                def transp_u(h, u):
                    for bi in range(njt // 8):
                        tb = 6 + bi % 2

                        def emit_ptr(e, bi=bi, tb=tb, u=u):
                            for i in range(8):
                                jt = bi * 8 + i
                                last = e.transpose(
                                    bank_b(tb)[:, i*128:(i+1)*128],
                                    Pb(u % 2)[:, jt*128:(jt+1)*128], idtb)
                            return last
                        rstep("pe", [f"Pb{u%2}", "idtb"], [f"bk{tb}"], emit_ptr)
                        rstep("dve", [f"bk{tb}"], PTRES,
                              lambda e, bi=bi, tb=tb, u=u, njt=njt:
                              e.tensor_copy(
                                  PT[:, u*njt*128 + bi*1024:
                                     u*njt*128 + bi*1024 + 1024], bank_b(tb)))

                softmax_u(h, 0)
                softmax_u(h, 1)
                transp_u(h, 0)
                softmax_u(h, 2)
                transp_u(h, 1)
                softmax_u(h, 3)
                transp_u(h, 2)
                transp_u(h, 3)
                pb = h % 2

                def emit_av(e, kvh=kvh, njt=njt, pb=pb, PTv=PTv):
                    for jt in range(njt):
                        last = e.matmul(bank_f(pb),
                                        VT(jt)[:, kvh*128:(kvh+1)*128],
                                        PTv[:, :, jt*128:(jt+1)*128],
                                        start=(jt == 0), stop=(jt == njt - 1))
                    return last
                rstep("pe", PTRES + ["VT"], [f"bk{pb}"], emit_av)
                rstep("act", [f"bk{pb}"], [f"AO{h}"],
                      lambda e, h=h, pb=pb: e.copy(AO(h), bank_f(pb)),
                      extra=[q_done])

            dump(f"AO_{half}", [AOall], [f"AO{i}" for i in range(NHEAD)])
            for dct in range(NB_D):
                wb = wsl(dct % 2)
                rdma("pool", [], [f"wsl{dct%2}"], [(wb, wo_t[dct], {})])
                pb = dct % 2

                def emit_o(e, wb=wb, pb=pb):
                    for ht in range(NHEAD):
                        last = e.matmul(bank_f(pb), wb[:, ht*128:(ht+1)*128],
                                        AO(ht)[:, 0:512],
                                        start=(ht == 0), stop=(ht == NHEAD - 1))
                    return last
                rstep("pe", [f"wsl{dct%2}"] + [f"AO{i}" for i in range(NHEAD)],
                      [f"bk{pb}"], emit_o)
                xo = xin(dct % 2)
                rdma("sp", [], [f"xin{dct%2}"],
                     [(xo[:, 0:512], xall[dct*128:(dct+1)*128, hc:hc+512], {})])
                sl = (dct % 2) * 512
                rstep("dve", [f"bk{pb}", f"xin{dct%2}"], ["stage"],
                      lambda e, xo=xo, pb=pb, sl=sl: e.tensor_add(
                          stage[:, sl:sl+512], bank_f(pb), xo[:, 0:512]))
                st_w = rdma("sp", ["stage"], [f"out_{dct}"],
                            [(out_d[dct*128:(dct+1)*128, half*512:half*512+512],
                              stage[:, sl:sl+512], {})])
                x1_steps[(half, dct)] = st_w

        bar = sch.step("sp", [Step(e, sch.counts[e]) for e in ENGS],
                       lambda e: e.nop())

        dump("x1", [out_d], [f"out_{i}" for i in range(NB_D)])
        for (zdram, z16t, G) in ((zg8_d, z16g, 32), (zu8_d, z16u, 32),
                                 (zd8_d, z16d, 86)):
            zn = f"z16_{G}_{id(z16t)}"
            rdma("sp", [], ["Mb2"], [(zb8, zdram, {})], extra=[bar])

            def emit_zx(e, z16t=z16t, G=G):
                xl = zb8[0:64, :].bitcast(U16).rearrange(
                    "p (a g) -> p a g", g=G // 2)
                xh = zb8[64:128, :].bitcast(U16).rearrange(
                    "p (a g) -> p a g", g=G // 2)
                zl = z16t[0:64, :].rearrange("p (a g) -> p a g", g=G)
                zh = z16t[64:128, :].rearrange("p (a g) -> p a g", g=G)
                e.tensor_scalar(zl[:, :, 0:G//2], xl, 15, None, ALU.bitwise_and)
                e.tensor_scalar(zl[:, :, G//2:G], xl, 8, 15,
                                ALU.logical_shift_right, ALU.bitwise_and)
                e.tensor_scalar(zh[:, :, 0:G//2], xh, 4, 15,
                                ALU.logical_shift_right, ALU.bitwise_and)
                return e.tensor_scalar(zh[:, :, G//2:G], xh, 12, None,
                                       ALU.logical_shift_right)
            rstep("dve", ["Mb2"], [zn], emit_zx)
        ZG = f"z16_32_{id(z16g)}"
        ZU = f"z16_32_{id(z16u)}"
        ZD = f"z16_86_{id(z16d)}"

        ln_stats(lambda dti: out_d[dti*128:(dti+1)*128, :], NB_D,
                 extra_fn=lambda dti: [bar, x1_steps[(0, dti)],
                                       x1_steps[(1, dti)],
                                       res(f"out_{dti}").w])
        ln_bcast(Mb2, Rb2, "Mb2", "Rb2")
        for dti in range(NB_D):
            xi = xin(dti % 2)
            rdma("sp", [f"out_{dti}"], [f"xin{dti%2}"],
                 [(xi, out_d[dti*128:(dti+1)*128, :], {})])
            rstep("pool", [f"xin{dti%2}", "Mb2", "stage"], ["stage"],
                  lambda e, xi=xi: e.tensor_sub(stage, xi, Mb2))
            s = rstep("pool", ["stage", "Rb2"], [XN2[dti]],
                      lambda e, dti=dti: e.tensor_mul(xn2(dti), stage, Rb2))
        ln2_done = s

        dump("xn2", [xn2(i) for i in range(NB_D)], XN2)
        dump("z16", [z16g, z16u, z16d], [ZG, ZU, ZD])

        def dequant_gu(slab_dram, zrun, zrname, srun, srname, a, q, wfi, Wmm,
                       which):
            si = a % 2
            sl = gslab(si)
            rdma("sp", [], [f"gsl{si}"], [(sl, slab_dram[a], {})])

            def emit_ex(e, sl=sl):
                x = sl[0:64, :].bitcast(U16)
                e.tensor_scalar(W16[0:64, 0:2048], x, 15, None, ALU.bitwise_and)
                e.tensor_scalar(W16[64:128, 0:2048], x, 4, 15,
                                ALU.logical_shift_right, ALU.bitwise_and)
                e.tensor_scalar(W16[0:64, 2048:4096], x, 8, 15,
                                ALU.logical_shift_right, ALU.bitwise_and)
                return e.tensor_scalar(W16[64:128, 2048:4096], x, 12, None,
                                       ALU.logical_shift_right)
            rstep("dve", [f"gsl{si}"], ["W16"], emit_ex)
            for hf in range(2):
                def emit_ts(e, hf=hf, q=q, wfi=wfi, zrun=zrun, srun=srun):
                    for i in range(16):
                        g = hf * 16 + i
                        zcol = q * 32 + int(GPOS_32[g])
                        last = e.tensor_scalar(
                            Wf(wfi)[:, i*128:(i+1)*128],
                            W16[:, g*128:(g+1)*128],
                            zrun[:, zcol:zcol+1], srun[:, q*32+g:q*32+g+1],
                            ALU.subtract, ALU.mult)
                    return last
                rstep("dve", ["W16", zrname, srname], WFRES(wfi), emit_ts)
                for bi in range(4):
                    tb = 6 + bi % 2

                    def emit_wtr(e, bi=bi, tb=tb, wfi=wfi):
                        for i in range(4):
                            last = e.transpose(
                                bank_f(tb)[:, i*128:(i+1)*128],
                                Wf(wfi)[:, (bi*4+i)*128:(bi*4+i+1)*128], idt)
                        return last
                    rstep("pe", WFRES(wfi) + ["idt"], [f"bk{tb}"], emit_wtr)
                    rstep("act", [f"bk{tb}"], [f"Wmm_{which}"],
                          lambda e, bi=bi, tb=tb, hf=hf, Wmm=Wmm: e.copy(
                              Wmm[:, hf*2048+bi*512:hf*2048+bi*512+512],
                              bank_f(tb)))

        sd_off = [0]
        for r, RLEN in enumerate(RUNS):
            L = RLEN * 128
            s0 = RUN_STARTS[r]
            rdma("sp", [], ["sgr", "sur", "sdr", "Mb"],
                 [(sgr[:, 0:RLEN*32], sg_d[:, s0*32:(s0+RLEN)*32], {}),
                  (sur[:, 0:RLEN*32], su_d[:, s0*32:(s0+RLEN)*32], {}),
                  (sdr[:, 0:32*RLEN], sd_d[:, sd_off[0]:sd_off[0]+32*RLEN], {})])
            rstep("dve", [ZG, "zgr", "Rb"], ["zgr"],
                  lambda e, s0=s0, RLEN=RLEN: e.tensor_copy(
                      zgr[:, 0:RLEN*32], z16g[:, s0*32:(s0+RLEN)*32]))
            rstep("dve", [ZU, "zur"], ["zur"],
                  lambda e, s0=s0, RLEN=RLEN: e.tensor_copy(
                      zur[:, 0:RLEN*32], z16u[:, s0*32:(s0+RLEN)*32]))
            sd_off[0] += 32 * RLEN
            for q in range(RLEN):
                a = s0 + q
                dequant_gu(g8, zgr, "zgr", sgr, "sgr", a, q, 0, Wgmm, "g")
                for tc in range(2):
                    def emit_g(e, tc=tc):
                        for g in range(32):
                            last = e.matmul(bank_f(tc),
                                            Wgmm[:, g*128:(g+1)*128],
                                            xn2(g)[:, tc*512:(tc+1)*512],
                                            start=(g == 0), stop=(g == 31))
                        return last
                    rstep("pe", ["Wmm_g"] + XN2, [f"bk{tc}"], emit_g,
                          extra=[ln2_done])
                    rstep("act", [f"bk{tc}"], ["gsilu"],
                          lambda e, tc=tc: e.activation(
                              gsilu[:, tc*512:(tc+1)*512], bank_f(tc), AF.Silu))
                dump(f"gs_{a}", [gsilu], ["gsilu"])
                dequant_gu(u8w, zur, "zur", sur, "sur", a, q, 1, Wumm, "u")
                for tc in range(2):
                    def emit_u(e, tc=tc):
                        for g in range(32):
                            last = e.matmul(bank_f(2+tc),
                                            Wumm[:, g*128:(g+1)*128],
                                            xn2(g)[:, tc*512:(tc+1)*512],
                                            start=(g == 0), stop=(g == 31))
                        return last
                    rstep("pe", ["Wmm_u"] + XN2, [f"bk{2+tc}"], emit_u)
                    rstep("dve", [f"bk{2+tc}", "gsilu"], [f"H{q}"],
                          lambda e, tc=tc, q=q: e.tensor_tensor(
                              Hb(q)[:, tc*512:(tc+1)*512], bank_f(2+tc),
                              gsilu[:, tc*512:(tc+1)*512], ALU.mult))
                dump(f"h_{r}_{q}", [Hb(q)], [f"H{q}"])
            for t in range(NB_D):
                si = t % 2
                sl = dslab(si)
                rdma("sp", [], [f"dsl{si}"],
                     [(sl[0:64, 0:L], d8[t][:, s0*128:s0*128+L], {})])

                def emit_exd(e, sl=sl, L=L):
                    x = sl[0:64, 0:L].bitcast(U16)
                    e.tensor_scalar(W16[0:64, 0:L//2], x, 15, None,
                                    ALU.bitwise_and)
                    e.tensor_scalar(W16[64:128, 0:L//2], x, 4, 15,
                                    ALU.logical_shift_right, ALU.bitwise_and)
                    e.tensor_scalar(W16[0:64, L//2:L], x, 8, 15,
                                    ALU.logical_shift_right, ALU.bitwise_and)
                    return e.tensor_scalar(W16[64:128, L//2:L], x, 12, None,
                                           ALU.logical_shift_right)
                rstep("dve", [f"dsl{si}"], ["W16"], emit_exd)

                def emit_zfd(e, t=t, s0=s0, RLEN=RLEN, si=si):
                    zfv = zdr(si)[:, 0:RLEN].rearrange(
                        "p (a two) -> p a two", two=2)
                    c_ev = t * 86 + s0 // 2
                    c_od = t * 86 + 43 + s0 // 2
                    e.tensor_copy(zfv[:, :, 0:1],
                                  z16d[:, c_ev:c_ev+RLEN//2].unsqueeze(2))
                    return e.tensor_copy(zfv[:, :, 1:2],
                                         z16d[:, c_od:c_od+RLEN//2].unsqueeze(2))
                rstep("dve", [ZD, f"zdr{si}"], [f"zdr{si}"], emit_zfd)
                for hf in range(2):
                    nblk = RLEN // 2

                    def emit_tsd(e, hf=hf, nblk=nblk, RLEN=RLEN, t=t, si=si):
                        for i in range(nblk):
                            q2 = hf * nblk + i
                            last = e.tensor_scalar(
                                Wf(si)[:, i*128:(i+1)*128],
                                W16[:, q2*128:(q2+1)*128],
                                zdr(si)[:, q2:q2+1],
                                sdr[:, t*RLEN+q2:t*RLEN+q2+1],
                                ALU.subtract, ALU.mult)
                        return last
                    rstep("dve", ["W16", f"zdr{si}", "sdr"], WFRES(si),
                          emit_tsd)
                    nb4 = (nblk + 3) // 4
                    for bi in range(nb4):
                        tb = 6 + bi % 2
                        lo = bi * 4
                        hi2 = min(lo + 4, nblk)

                        def emit_wtrd(e, lo=lo, hi2=hi2, tb=tb, si=si):
                            for i in range(hi2 - lo):
                                last = e.transpose(
                                    bank_f(tb)[:, i*128:(i+1)*128],
                                    Wf(si)[:, (lo+i)*128:(lo+i+1)*128], idt)
                            return last
                        rstep("pe", WFRES(si) + ["idt"], [f"bk{tb}"],
                              emit_wtrd)
                        rstep("act", [f"bk{tb}"], ["Wmm_d"],
                              lambda e, lo=lo, hi2=hi2, tb=tb, hf=hf, nblk=nblk:
                              e.copy(Wdmm[:, (hf*nblk+lo)*128:
                                          (hf*nblk+lo)*128+(hi2-lo)*128],
                                     bank_f(tb)[:, 0:(hi2-lo)*128]))
                dump(f"wd_{r}_{t}", [Wdmm[:, 0:L]], ["Wmm_d"])
                for tc in range(2):
                    def emit_d(e, tc=tc, RLEN=RLEN):
                        for q2 in range(RLEN):
                            last = e.matmul(bank_f(4+tc),
                                            Wdmm[:, q2*128:(q2+1)*128],
                                            Hb(q2)[:, tc*512:(tc+1)*512],
                                            start=(q2 == 0),
                                            stop=(q2 == RLEN - 1))
                        return last
                    rstep("pe", ["Wmm_d"] + [f"H{i}" for i in range(RLEN)],
                          [f"bk{4+tc}"], emit_d)
                    rstep("dve", [f"bk{4+tc}"], ["stage"],
                          lambda e, tc=tc: e.tensor_copy(
                              stage[:, tc*512:(tc+1)*512], bank_f(4+tc)))
                dump(f"dstage_{r}_{t}", [stage], ["stage"])
                rdma("pool", ["stage"], [f"out_{t}"],
                     [(out_d[t*128:(t+1)*128, :], stage,
                       {"accum_op": ALU.add})], extra=[ln2_done])
            dump(f"outmid_{r}", [out_d], [f"out_{i}" for i in range(NB_D)])

        sch.step("sp", [Step("pool", sch.counts["pool"]),
                        Step("sp", sch.counts["sp"])], lambda e: e.nop())

        block = stack.enter_context(nc.Block())
        sch.replay(block)
    return nc


_NC_CACHE = []


def kernel(L0_attn_x, L0_attn_wq, L0_attn_wk, L0_attn_wv, L0_attn_wo,
           L0_ffn_gate_qweight, L0_ffn_gate_scales, L0_ffn_gate_qzeros,
           L0_ffn_up_qweight, L0_ffn_up_scales, L0_ffn_up_qzeros,
           L0_ffn_down_qweight, L0_ffn_down_scales, L0_ffn_down_qzeros):
    x = np.asarray(L0_attn_x, dtype=np.float32)
    wq_t = _tile_kpart(np.asarray(L0_attn_wq, dtype=np.float32))
    wk_t = _tile_kpart(np.asarray(L0_attn_wk, dtype=np.float32))
    wv_t = _tile_kpart(np.asarray(L0_attn_wv, dtype=np.float32))
    wo_t = _tile_wo(np.asarray(L0_attn_wo, dtype=np.float32))
    g8 = _gateup_slabs(np.asarray(L0_ffn_gate_qweight))
    u8w = _gateup_slabs(np.asarray(L0_ffn_up_qweight))
    d8 = _down_slabs(np.asarray(L0_ffn_down_qweight))
    sg = _gateup_scales(np.asarray(L0_ffn_gate_scales))
    su = _gateup_scales(np.asarray(L0_ffn_up_scales))
    sd = _down_scales(np.asarray(L0_ffn_down_scales))
    zg8 = _gateup_zero_bytes(np.asarray(L0_ffn_gate_qzeros))
    zu8 = _gateup_zero_bytes(np.asarray(L0_ffn_up_qzeros))
    zd8 = _down_zero_bytes(np.asarray(L0_ffn_down_qzeros))
    idt = np.eye(128, dtype=np.float32)
    idtb = np.eye(128).astype(ml_dtypes.bfloat16)

    xt = np.ascontiguousarray(x.transpose(0, 2, 1))[:, PI_D, :]

    in_maps = []
    core_meta = []
    for c in range(8):
        b, par = c // 2, c % 2
        if par == 0:
            ownX, restH0 = (0, 512), (512, 1024)
            ownY, restH1 = (1536, 2048), (1024, 1536)
        else:
            ownX, restH0 = (512, 1024), (0, 512)
            ownY, restH1 = (1024, 1536), (1536, 2048)
        colord = np.concatenate([np.arange(*ownX), np.arange(*restH0),
                                 np.arange(*ownY), np.arange(*restH1)])
        xc = np.ascontiguousarray(xt[b][:, colord])
        jabs = colord
        maskX = np.empty((4, 128, 1024), dtype=ml_dtypes.bfloat16)
        maskY = np.empty((4, 128, 2048), dtype=ml_dtypes.bfloat16)
        for u in range(4):
            qX = ownX[0] + 128 * u + np.arange(128)[:, None]
            maskX[u] = (jabs[None, :1024] <= qX).astype(ml_dtypes.bfloat16)
            qY = ownY[0] + 128 * u + np.arange(128)[:, None]
            maskY[u] = (jabs[None, :2048] <= qY).astype(ml_dtypes.bfloat16)
        in_maps.append(dict(
            xall=xc, wq_t=wq_t, wk_t=wk_t, wv_t=wv_t, wo_t=wo_t,
            g8=g8, u8w=u8w, d8=d8, sg=sg, su=su, sd=sd,
            zg8=zg8, zu8=zu8, zd8=zd8, maskX=maskX, maskY=maskY,
            idt=idt, idtb=idtb))
        core_meta.append((b, ownX, ownY))

    if not _NC_CACHE:
        _NC_CACHE.append(build_program())
    nc = _NC_CACHE[0]
    resobj = run_bass_kernel_spmd(nc, in_maps, core_ids=list(range(8)))

    out = np.empty((4, 2048, 4096), dtype=np.float32)
    for c in range(8):
        b, ownX, ownY = core_meta[c]
        o = np.asarray(resobj.results[c]["out"])
        out[b, ownX[0]:ownX[1], PI_D] = o[:, 0:512]
        out[b, ownY[0]:ownY[1], PI_D] = o[:, 512:1024]
    return out




# version 12
# speedup vs baseline: 1.8078x; 1.4180x over previous
import numpy as np
import ml_dtypes
from contextlib import ExitStack
import concourse.bass as bass
import concourse.mybir as mybir
from concourse.bass_utils import run_bass_kernel_spmd

F32 = mybir.dt.float32
BF16 = mybir.dt.bfloat16
U8 = mybir.dt.uint8
U16 = mybir.dt.uint16
ALU = mybir.AluOpType
AF = mybir.ActivationFunctionType
AX = mybir.AxisListType

D = 4096
DFF = 11008
S = 2048
NB_D = 32
NB_N = 86
RUNS = [16, 16, 16, 16, 16, 6]
RUN_STARTS = [0, 16, 32, 48, 64, 80]
SCALE = 1.0 / (128.0 ** 0.5)
NHEAD = 32
NKV = 8

_LUTLO = np.array([0, 4, 1, 5])
_LUTHI = np.array([2, 6, 3, 7])


def _canonical_perm(N):
    pi = np.empty(N, dtype=np.int64)
    a = np.arange(N // 128)
    for p in range(128):
        if p < 64:
            m = 64 * a + p
            pi[a * 128 + p] = 8 * (m // 4) + _LUTLO[p % 4]
        else:
            m = 64 * a + (p - 64)
            pi[a * 128 + p] = 8 * (m // 4) + _LUTHI[(p - 64) % 4]
    return pi


PI_D = _canonical_perm(D)
PI_N = _canonical_perm(DFF)
GPOS_32 = np.argsort(np.concatenate([np.arange(0, 32, 2), np.arange(1, 32, 2)]))
GPOS_86 = np.argsort(np.concatenate([np.arange(0, 86, 2), np.arange(1, 86, 2)]))


def _interleave_idx(L):
    ki = np.empty(L, dtype=np.int64)
    ki[0::2] = np.arange(L // 2)
    ki[1::2] = L // 2 + np.arange(L // 2)
    return ki


def _gateup_slabs(qweight):
    qb = np.ascontiguousarray(qweight).view(np.uint8).reshape(D, DFF // 2)
    qb = qb[PI_D, :][_interleave_idx(D), :]
    return np.ascontiguousarray(qb.reshape(D, NB_N, 64).transpose(1, 2, 0))


def _down_slabs(qweight):
    db = np.ascontiguousarray(qweight).view(np.uint8).reshape(DFF, D // 2)
    db = db[PI_N, :]
    out = np.empty((NB_D, 64, DFF), dtype=np.uint8)
    for r, R in enumerate(RUNS):
        lo = RUN_STARTS[r] * 128
        L = R * 128
        sub = db[lo + _interleave_idx(L), :]
        out[:, :, lo:lo + L] = sub.reshape(L, NB_D, 64).transpose(1, 2, 0)
    return np.ascontiguousarray(out)


def _gateup_scales(scales):
    sc = scales[:, PI_N]
    t = sc.T.reshape(NB_N, 128, 32).transpose(1, 0, 2).reshape(128, NB_N * 32)
    return np.ascontiguousarray(t.astype(np.float32))


def _down_scales(scales):
    sc = scales[:, PI_D]
    t = sc.T.reshape(NB_D, 128, 86).transpose(1, 0, 2)
    out = np.empty((128, NB_D * 86), dtype=np.float32)
    off = 0
    for r, R in enumerate(RUNS):
        s0 = RUN_STARTS[r]
        blk = t[:, :, s0:s0 + R]
        out[:, off:off + 32 * R] = blk.reshape(128, 32 * R)
        off += 32 * R
    return np.ascontiguousarray(out)


def _gateup_scales_zlayout(scales):
    t = _gateup_scales(scales)
    out = np.empty_like(t)
    for g in range(32):
        out[:, GPOS_32[g]::32] = 0
    cols = np.arange(NB_N)[:, None] * 32 + GPOS_32[None, :]
    src_cols = np.arange(NB_N)[:, None] * 32 + np.arange(32)[None, :]
    out[:, cols.ravel()] = t[:, src_cols.ravel()]
    return np.ascontiguousarray(out)


def _gateup_zero_bytes(qzeros):
    zb = np.ascontiguousarray(qzeros).view(np.uint8).reshape(32, DFF // 2).T
    h = np.empty((128, NB_N * 32), dtype=np.uint8)
    for a in range(NB_N):
        blk = zb[64 * a:64 * a + 64, :]
        h[0:64, a * 32:(a + 1) * 32] = blk
        h[64:128, a * 32:(a + 1) * 32] = blk
    return h


def _down_zero_bytes(qzeros):
    zb = np.ascontiguousarray(qzeros).view(np.uint8).reshape(86, D // 2).T
    h = np.empty((128, NB_D * 86), dtype=np.uint8)
    for t in range(NB_D):
        blk = zb[64 * t:64 * t + 64, :]
        h[0:64, t * 86:(t + 1) * 86] = blk
        h[64:128, t * 86:(t + 1) * 86] = blk
    return h


def _tile_kpart(w):
    wc = w[PI_D, :]
    C = w.shape[1]
    t = wc.reshape(NB_D, 128, C // 128, 128).transpose(2, 1, 0, 3)
    return np.ascontiguousarray(t.reshape(C // 128, 128, NB_D * 128)
                                .astype(np.float32))


def _tile_wo(wo):
    wc = wo[:, PI_D]
    t = wc.reshape(32, 128, NB_D, 128).transpose(2, 1, 0, 3)
    return np.ascontiguousarray(t.reshape(NB_D, 128, 32 * 128).astype(np.float32))


ENGS = ("pe", "dve", "act", "pool", "sp")


class Step:
    __slots__ = ("eng", "val")
    def __init__(self, eng, val):
        self.eng = eng
        self.val = val


class Res:
    __slots__ = ("w", "rs")
    def __init__(self):
        self.w = None
        self.rs = []
    def rd(self):
        return [self.w] if self.w is not None else []
    def wd(self):
        d = list(self.rs)
        if self.w is not None:
            d.append(self.w)
        return d
    def read(self, s):
        self.rs.append(s)
    def wrote(self, s):
        self.w = s
        self.rs = []


class Sched:
    def __init__(self, nc, stack):
        self.nc = nc
        self.sems = {e: stack.enter_context(nc.semaphore(f"sem_{e}"))
                     for e in ENGS}
        self.counts = {e: 0 for e in ENGS}
        self.items = []
        self.covered = {e: {} for e in ENGS}

    NO_DEDUPE = True

    def _waits(self, eng, deps):
        cov = self.covered[eng]
        out = []
        seen = {}
        for d in deps:
            if d is None:
                continue
            if d.eng == eng and eng in ("pe", "sp"):
                cov[eng] = max(cov.get(eng, -1), d.val)
                continue
            if not self.NO_DEDUPE and cov.get(d.eng, -1) >= d.val:
                continue
            if seen.get(d.eng, -1) >= d.val:
                continue
            seen[d.eng] = d.val
            cov[d.eng] = max(cov.get(d.eng, -1), d.val)
        for k, v in seen.items():
            out.append((k, v))
        return out

    def step(self, eng, deps, emit):
        waits = self._waits(eng, deps)
        self.items.append((eng, waits, emit, False))
        self.counts[eng] += 1
        if eng in ("pe", "sp"):
            self.covered[eng][eng] = self.counts[eng]
        return Step(eng, self.counts[eng])

    def dma(self, eng, deps, pairs):
        waits = self._waits(eng, deps)
        self.items.append((eng, waits, list(pairs), True))
        self.counts[eng] += 16 * len(pairs)
        if eng in ("pe", "sp"):
            self.covered[eng][eng] = self.counts[eng]
        return Step(eng, self.counts[eng])

    def replay(self, block):
        sched = self

        def body(eng_key):
            def _(e):
                for (eng, waits, emit, is_dma) in sched.items:
                    if eng != eng_key:
                        continue
                    for (we, wv) in waits:
                        e.wait_ge(sched.sems[we], wv)
                    if is_dma:
                        for (o, i, kw) in emit:
                            e.dma_start(out=o, in_=i, **kw).then_inc(
                                sched.sems[eng], 16)
                    else:
                        last = emit(e)
                        last.then_inc(sched.sems[eng], 1)
            return _
        block.tensor(body("pe"))
        block.vector(body("dve"))
        block.scalar(body("act"))
        block.gpsimd(body("pool"))
        block.sync(body("sp"))


DEBUG_DUMPS = {}
PIPE_AO = True
PIPE_H = True
PIPE_D = True


def build_program():
    nc = bass.Bass("TRN2", target_bir_lowering=False)

    def din(name, shape, dt):
        return nc.dram_tensor(name, shape, dt, kind="ExternalInput").ap()

    xall = din("xall", [D, S], F32)
    wq_t = din("wq_t", [32, 128, D], F32)
    wk_t = din("wk_t", [8, 128, D], F32)
    wv_t = din("wv_t", [8, 128, D], F32)
    wo_t = din("wo_t", [32, 128, D], F32)
    g8 = din("g8", [NB_N, 64, D], U8)
    u8w = din("u8w", [NB_N, 64, D], U8)
    d8 = din("d8", [NB_D, 64, DFF], U8)
    sg_d = din("sg", [128, NB_N * 32], F32)
    su_d = din("su", [128, NB_N * 32], F32)
    suz_d = din("suz", [128, NB_N * 32], F32)
    sd_d = din("sd", [128, NB_D * 86], F32)
    zg8_d = din("zg8", [128, NB_N * 32], U8)
    zu8_d = din("zu8", [128, NB_N * 32], U8)
    zd8_d = din("zd8", [128, NB_D * 86], U8)
    maskX_d = din("maskX", [4, 128, 1024], BF16)
    maskY_d = din("maskY", [4, 128, 2048], BF16)
    idt_d = din("idt", [128, 128], F32)
    idtb_d = din("idtb", [128, 128], BF16)
    out_d = nc.dram_tensor("out", [D, 1024], F32, kind="ExternalOutput").ap()

    stack = ExitStack()
    with stack:
        AR_BYTES = 212800
        arena_t = stack.enter_context(nc.sbuf_tensor("arena", [128, AR_BYTES], U8))
        arena = arena_t[:]
        _off = [0]

        def alloc(nbytes, align=64):
            o = (_off[0] + align - 1) // align * align
            _off[0] = o + nbytes
            assert _off[0] <= AR_BYTES, f"arena overflow {_off[0]}"
            return o

        def vw(off, dt, rows, cols):
            esz = {F32: 4, BF16: 2, U8: 1, U16: 2}[dt]
            ap = arena[0:rows, off:off + cols * esz]
            return ap.bitcast(dt) if dt != U8 else ap

        o_idt = alloc(512); idt = vw(o_idt, F32, 128, 128)
        o_idtb = alloc(256); idtb = vw(o_idtb, BF16, 128, 128)
        o_ones = alloc(64); ones = vw(o_ones, F32, 128, 1)
        o_vec = alloc(2 * 4096)

        def vec(i):
            return vw(o_vec + i * 4096, F32, 1, 1024)
        o_stage = alloc(1024 * 4); stage = vw(o_stage, F32, 128, 1024)
        o_xin = alloc(2 * 1024 * 4)

        def xin(i):
            return vw(o_xin + i * 4096, F32, 128, 1024)
        o_mb = alloc(1024 * 4); Mb = vw(o_mb, F32, 128, 1024)
        o_rb = alloc(1024 * 4); Rb = vw(o_rb, F32, 128, 1024)
        o_mx = alloc(128)
        def mxbuf(i):
            return vw(o_mx + i * 64, F32, 128, 4)
        o_ngm = alloc(128)
        def negmax(i):
            return vw(o_ngm + i * 64, F32, 128, 1)
        o_zrow = alloc(128)
        def zrow(i):
            return vw(o_zrow + i * 64, F32, 128, 1)
        o_rz = alloc(128)
        def rz(i):
            return vw(o_rz + i * 64, F32, 128, 1)

        o_big = (_off[0] + 63) // 64 * 64

        oA = [o_big]

        def allocA(nbytes, align=64):
            o = (oA[0] + align - 1) // align * align
            oA[0] = o + nbytes
            assert oA[0] <= AR_BYTES, f"phase A overflow {oA[0]}"
            return o
        o_xn1 = allocA(32 * 1024)

        def xn1(i):
            return vw(o_xn1 + i * 1024, BF16, 128, 512)
        o_ao = o_xn1
        AOall = vw(o_ao, BF16, 128, 16384)

        def AO(h):
            return vw(o_ao + h * 1024, BF16, 128, 512)
        o_k = allocA(8 * 4096)

        def Kb(kvh):
            return vw(o_k + kvh * 4096, BF16, 128, 2048)
        o_vt = allocA(16 * 2048)
        VTall = vw(o_vt, BF16, 128, 16384)

        def VT(jt):
            return vw(o_vt + jt * 2048, BF16, 128, 1024)
        o_q = allocA(32 * 1024)

        def Qs(h):
            return vw(o_q + h * 1024, BF16, 128, 512)
        o_msk = allocA(16384)

        def maskv(u, E):
            return vw(o_msk + u * 2 * E, BF16, 128, E)
        o_pb = allocA(2 * 4096)

        def Pb(i):
            return vw(o_pb + i * 4096, BF16, 128, 2048)
        o_ws = allocA(2 * 8192)

        def wsl(i):
            return vw(o_ws + i * 8192, BF16, 128, 4096)
        o_pt = o_ws
        PTRES = ["wsl0", "wsl1"]

        def PTh(njt):
            return vw(o_pt, BF16, 128, 4 * njt * 128)
        Vtmp = vw(o_pb, BF16, 128, 512)

        oB = [o_big]

        def allocB(nbytes, align=64):
            o = (oB[0] + align - 1) // align * align
            oB[0] = o + nbytes
            assert oB[0] <= AR_BYTES, f"phase B overflow {oB[0]}"
            return o
        o_xn2 = allocB(32 * 2048)

        def xn2(i):
            return vw(o_xn2 + i * 2048, BF16, 128, 1024)
        o_h = allocB(16 * 2048)

        def Hb(i):
            return vw(o_h + i * 2048, BF16, 128, 1024)
        o_wgmm = allocB(8192); Wgmm = vw(o_wgmm, BF16, 128, 4096)
        o_wumm = allocB(8192); Wumm = vw(o_wumm, BF16, 128, 4096)
        o_wdmm = allocB(4096); Wdmm = vw(o_wdmm, BF16, 128, 2048)
        o_gsilu = allocB(2048); gsilu = vw(o_gsilu, BF16, 128, 1024)
        o_w16 = allocB(8192); W16 = vw(o_w16, U16, 128, 4096)
        o_wf = allocB(8192)
        o_gsl = allocB(2 * 4096)

        def gslab(i):
            return vw(o_gsl + i * 4096, U8, 64, 4096)
        o_dsl = allocB(2 * 2048)

        def dslab(i):
            return vw(o_dsl + i * 2048, U8, 64, 2048)
        o_z16 = allocB(3 * 5504)
        z16g = vw(o_z16, U16, 128, 2752)
        z16u = vw(o_z16 + 5504, U16, 128, 2752)
        z16d = vw(o_z16 + 11008, U16, 128, 2752)
        sgr = vw(o_mb, F32, 128, 512)
        sur = vw(o_mb + 2048, F32, 128, 512)
        zgr = vw(o_rb, F32, 128, 512)
        zur = vw(o_rb + 2048, F32, 128, 512)
        o_sdr = allocB(512 * 4); sdr = vw(o_sdr, F32, 128, 512)
        o_suz = allocB(512 * 4); suzr = vw(o_suz, F32, 128, 512)
        o_zdr = allocB(2 * 64)

        def zdr(i):
            return vw(o_zdr + i * 64, F32, 128, 16)
        o_mb2 = allocB(1024 * 4); Mb2 = vw(o_mb2, F32, 128, 1024)
        o_rb2 = allocB(1024 * 4); Rb2 = vw(o_rb2, F32, 128, 1024)
        zb8 = vw(o_mb2, U8, 128, 2752)
        def Wf(i):
            return vw(o_wf if i == 0 else o_mb2, F32, 128, 2048)
        def WFRES(i):
            return ["Wf0"] if i == 0 else ["Mb2", "Rb2"]
        Wfb = vw(o_mb2, BF16, 128, 2048)
        zsur = vw(o_rb + 2048, F32, 128, 512)

        psf = [stack.enter_context(nc.psum_tensor(f"ps{i}", [128, 512], F32))
               for i in range(6)]
        psb = [stack.enter_context(nc.psum_tensor(f"pb{i}", [128, 1024], BF16))
               for i in range(2)]

        def bank_f(i):
            return psf[i][:] if i < 6 else psb[i - 6][:].bitcast(F32)

        def bank_b(i):
            return psb[i - 6][:]

        sch = Sched(nc, stack)
        R = {}

        def res(name):
            if name not in R:
                R[name] = Res()
            return R[name]

        def rstep(eng, reads, writes, emit, extra=()):
            deps = list(extra)
            for n in reads:
                deps += res(n).rd()
            for n in writes:
                deps += res(n).wd()
            s = sch.step(eng, deps, emit)
            for n in reads:
                res(n).read(s)
            for n in writes:
                res(n).wrote(s)
            return s

        def rdma(q, reads, writes, pairs, extra=()):
            deps = list(extra)
            for n in reads:
                deps += res(n).rd()
            for n in writes:
                deps += res(n).wd()
            s = sch.dma(q, deps, pairs)
            for n in reads:
                res(n).read(s)
            for n in writes:
                res(n).wrote(s)
            return s

        XN1 = [f"xn1_{i}" for i in range(NB_D)]
        XN2 = [f"xn2_{i}" for i in range(NB_D)]

        def dump(name, aps, reads):
            if name not in DEBUG_DUMPS:
                return
            pairs = []
            for i, ap in enumerate(aps):
                dt = ap.dtype
                dd = nc.dram_tensor(f"dump_{name}_{i}", list(ap.shape), dt,
                                    kind="ExternalOutput").ap()
                pairs.append((dd, ap, {}))
            allb = [Step(e, sch.counts[e]) for e in ENGS]
            rdma("sp", reads, [], pairs, extra=allb)

        rdma("sp", [], ["idt", "idtb"], [(idt, idt_d, {}), (idtb, idtb_d, {})])
        rstep("dve", [], ["ones"], lambda e: e.memset(ones, 1.0))

        def ln_stats(src_ap_fn, n_dt, extra_fn=None):
            for dti in range(n_dt):
                xi = xin(dti % 2)
                ex = extra_fn(dti) if extra_fn else []
                rdma("sp", [], [f"xin{dti%2}"], [(xi, src_ap_fn(dti), {})],
                     extra=ex)
                rstep("act", [f"xin{dti%2}"], ["stage"],
                      lambda e, xi=xi: e.activation(stage, xi, AF.Square))

                def emit_stats(e, xi=xi, dti=dti, n_dt=n_dt):
                    first = (dti == 0)
                    last = (dti == n_dt - 1)
                    e.matmul(bank_f(2)[0:1, :], ones, xi[:, 0:512],
                             start=first, stop=last)
                    e.matmul(bank_f(3)[0:1, :], ones, xi[:, 512:1024],
                             start=first, stop=last)
                    e.matmul(bank_f(4)[0:1, :], ones, stage[:, 0:512],
                             start=first, stop=last)
                    return e.matmul(bank_f(5)[0:1, :], ones, stage[:, 512:1024],
                                    start=first, stop=last)
                rstep("pe", [f"xin{dti%2}", "stage"],
                      ["bk2", "bk3", "bk4", "bk5"], emit_stats)

            def emit_cp(e):
                e.tensor_copy(vec(0)[0:1, 0:512], bank_f(2)[0:1, :])
                e.tensor_copy(vec(0)[0:1, 512:1024], bank_f(3)[0:1, :])
                e.tensor_copy(vec(1)[0:1, 0:512], bank_f(4)[0:1, :])
                return e.tensor_copy(vec(1)[0:1, 512:1024], bank_f(5)[0:1, :])
            rstep("dve", ["bk2", "bk3", "bk4", "bk5"], ["vec0", "vec1"], emit_cp)
            rstep("dve", [], ["vec0"],
                  lambda e: e.tensor_scalar(vec(0), vec(0), 1.0 / D, None,
                                            ALU.mult))
            rstep("dve", [], ["vec1"],
                  lambda e: e.tensor_scalar(vec(1), vec(1), 1.0 / D, None,
                                            ALU.mult))
            rstep("act", ["vec0"], ["stage"],
                  lambda e: e.activation(stage[0:1, :], vec(0), AF.Square))
            rstep("dve", ["stage"], ["vec1"],
                  lambda e: e.scalar_tensor_tensor(vec(1), vec(1), 1e-5,
                                                   stage[0:1, :],
                                                   ALU.add, ALU.subtract))
            rstep("act", [], ["vec1"],
                  lambda e: e.activation(vec(1), vec(1), AF.Ln))
            rstep("act", [], ["vec1"],
                  lambda e: e.activation(vec(1), vec(1), AF.Exp, scale=-0.5))

        def ln_bcast(Mt, Rt, mname, rname):
            def emit_bc(e):
                onesrow = ones[0:1, 0:1].broadcast_to([1, 128])
                e.matmul(bank_f(0), onesrow, vec(0)[0:1, 0:512],
                         start=True, stop=True)
                e.matmul(bank_f(1), onesrow, vec(0)[0:1, 512:1024],
                         start=True, stop=True)
                e.matmul(bank_f(2), onesrow, vec(1)[0:1, 0:512],
                         start=True, stop=True)
                return e.matmul(bank_f(3), onesrow, vec(1)[0:1, 512:1024],
                                start=True, stop=True)
            rstep("pe", ["vec0", "vec1"], ["bk0", "bk1", "bk2", "bk3"], emit_bc)

            def emit_mbrb(e):
                e.tensor_copy(Mt[:, 0:512], bank_f(0))
                e.tensor_copy(Mt[:, 512:1024], bank_f(1))
                e.tensor_copy(Rt[:, 0:512], bank_f(2))
                return e.tensor_copy(Rt[:, 512:1024], bank_f(3))
            rstep("dve", ["bk0", "bk1", "bk2", "bk3"], [mname, rname], emit_mbrb)

        x1_steps = {}
        for half in range(2):
            hc = half * 1024
            E = 2048 if half else 1024
            nch = E // 512
            njt = E // 128
            mk_d = maskY_d if half else maskX_d

            ln_stats(lambda dti: xall[dti*128:(dti+1)*128, hc:hc+1024], NB_D)
            ln_bcast(Mb, Rb, "Mb", "Rb")

            for qt in range(2):
                qc = hc + qt * 512
                for dti in range(NB_D):
                    xi = xin(dti % 2)
                    rdma("sp", [], [f"xin{dti%2}"],
                         [(xi[:, 0:512],
                           xall[dti*128:(dti+1)*128, qc:qc+512], {})])
                    rstep("pool", [f"xin{dti%2}", "Mb"], ["stage"],
                          lambda e, xi=xi, qt=qt: e.tensor_sub(
                              stage[:, 0:512], xi[:, 0:512],
                              Mb[:, qt*512:qt*512+512]))
                    alias_w = ([f"AO{i}" for i in range(NHEAD)]
                               if (half == 1 and qt == 0) else [])
                    rstep("pool", ["stage", "Rb"], [XN1[dti]] + alias_w,
                          lambda e, dti=dti, qt=qt: e.tensor_mul(
                              xn1(dti), stage[:, 0:512],
                              Rb[:, qt*512:qt*512+512]))

                for ct in range(16):
                    is_v = ct >= 8
                    kvh = ct % 8
                    wb = wsl(ct % 2)
                    rdma("pool", [], [f"wsl{ct%2}"],
                         [(wb, (wv_t if is_v else wk_t)[kvh], {})])
                    pb = ct % 2

                    def emit_kv(e, wb=wb, pb=pb):
                        for dti in range(NB_D):
                            last = e.matmul(
                                bank_f(pb), wb[:, dti*128:(dti+1)*128],
                                xn1(dti)[:, 0:512],
                                start=(dti == 0), stop=(dti == NB_D - 1))
                        return last
                    rstep("pe", [f"wsl{ct%2}"] + XN1, [f"bk{pb}"], emit_kv)
                    if not is_v:
                        rstep("act", [f"bk{pb}"], [f"K{kvh}"],
                              lambda e, kvh=kvh, pb=pb, qc=qc:
                              e.copy(Kb(kvh)[:, qc:qc+512], bank_f(pb)))
                    else:
                        rstep("act", [f"bk{pb}"], ["PT"],
                              lambda e, pb=pb: e.copy(Vtmp, bank_f(pb)))
                        tb = 6 + ct % 2

                        def emit_vtr(e, tb=tb):
                            for i in range(4):
                                last = e.transpose(
                                    bank_b(tb)[:, i*128:(i+1)*128],
                                    Vtmp[:, i*128:(i+1)*128], idtb)
                            return last
                        rstep("pe", ["PT", "idtb"], [f"bk{tb}"], emit_vtr)
                        jt0 = qc // 128

                        def emit_vev(e, tb=tb, jt0=jt0, kvh=kvh):
                            dst = VTall.rearrange("p (j c) -> p j c", j=16)[
                                :, jt0:jt0+4, kvh*128:(kvh+1)*128]
                            src = bank_b(tb)[:, 0:512].rearrange(
                                "p (j c) -> p j c", j=4)
                            return e.tensor_copy(dst, src)
                        rstep("dve", [f"bk{tb}"], ["VT"], emit_vev)

                dump(f"xn1_{half}_{qt}", [xn1(i) for i in range(NB_D)], XN1)
                dump(f"K_{half}_{qt}", [Kb(k) for k in range(8)],
                     [f"K{k}" for k in range(8)])
                dump(f"VT_{half}_{qt}", [VTall], ["VT"])
                if qt == 0:
                    for h in range(NHEAD):
                        wb = wsl(h % 2)
                        rdma("pool", [], [f"wsl{h%2}"], [(wb, wq_t[h], {})])
                        pb = h % 2

                        def emit_q(e, wb=wb, pb=pb):
                            for dti in range(NB_D):
                                last = e.matmul(
                                    bank_f(pb), wb[:, dti*128:(dti+1)*128],
                                    xn1(dti)[:, 0:512],
                                    start=(dti == 0), stop=(dti == NB_D - 1))
                            return last
                        rstep("pe", [f"wsl{h%2}"] + XN1, [f"bk{pb}"], emit_q)
                        q_done = rstep("act", [f"bk{pb}"], [f"Q{h}"],
                                       lambda e, h=h, pb=pb: e.mul(
                                           Qs(h), bank_f(pb), SCALE))

            dump(f"Q_{half}", [Qs(h) for h in range(NHEAD)],
                 [f"Q{h}" for h in range(NHEAD)])
            rdma("sp", [], ["mask"],
                 [(maskv(u, E), mk_d[u], {}) for u in range(4)],
                 extra=[q_done])
            PT = PTh(njt)
            PTv = PT.rearrange("p (u c) -> p u c", u=4)
            pend_ao = [None]

            def flush_ao():
                if pend_ao[0] is None:
                    return
                hh2, pb2 = pend_ao[0]
                rstep("act", [f"bk{pb2}"], [f"AO{hh2}"],
                      lambda e, hh2=hh2, pb2=pb2: e.copy(AO(hh2), bank_f(pb2)),
                      extra=[q_done])
                pend_ao[0] = None

            for h in range(NHEAD):
                kvh = h // 4

                def softmax_u(h, u):
                    p = (h * 4 + u) % 2

                    def emit_sc(e, h=h, u=u, kvh=kvh, nch=nch):
                        for c in range(nch):
                            last = e.matmul(bank_f(2 + c),
                                            Qs(h)[:, u*128:(u+1)*128],
                                            Kb(kvh)[:, c*512:(c+1)*512],
                                            start=True, stop=True)
                        return last
                    rstep("pe", [f"Q{h}", f"K{kvh}"],
                          [f"bk{2+c}" for c in range(nch)], emit_sc)

                    def emit_exp(e, nch=nch, u=u):
                        for c in range(nch):
                            last = e.activation(Pb(u % 2)[:, c*512:(c+1)*512],
                                                bank_f(2 + c), AF.Exp)
                        return last
                    rstep("act", [f"bk{2+c}" for c in range(nch)],
                          [f"Pb{u%2}"], emit_exp)
                    rstep("dve", [f"Pb{u%2}", "mask"], [f"Pb{u%2}", f"zr{p}"],
                          lambda e, E=E, u=u, p=p: e.scalar_tensor_tensor(
                              Pb(u % 2)[:, 0:E], Pb(u % 2)[:, 0:E], 1.0,
                              maskv(u, E), ALU.mult, ALU.mult,
                              accum_out=zrow(p)))
                    rstep("act", [f"zr{p}"], [f"rz{p}"],
                          lambda e, p=p: e.activation(rz(p), zrow(p), AF.Ln))
                    rstep("act", [], [f"rz{p}"],
                          lambda e, p=p: e.activation(rz(p), rz(p), AF.Exp,
                                                      scale=-1.0))
                    rstep("dve", [f"rz{p}"], [f"Pb{u%2}"],
                          lambda e, E=E, u=u, p=p: e.tensor_scalar(
                              Pb(u % 2)[:, 0:E], Pb(u % 2)[:, 0:E], rz(p), None,
                              ALU.mult))

                def transp_u(h, u):
                    for bi in range(njt // 8):
                        tb = 6 + bi % 2

                        def emit_ptr(e, bi=bi, tb=tb, u=u):
                            for i in range(8):
                                jt = bi * 8 + i
                                last = e.transpose(
                                    bank_b(tb)[:, i*128:(i+1)*128],
                                    Pb(u % 2)[:, jt*128:(jt+1)*128], idtb)
                            return last
                        rstep("pe", [f"Pb{u%2}", "idtb"], [f"bk{tb}"], emit_ptr)
                        rstep("dve", [f"bk{tb}"], PTRES,
                              lambda e, bi=bi, tb=tb, u=u, njt=njt:
                              e.tensor_copy(
                                  PT[:, u*njt*128 + bi*1024:
                                     u*njt*128 + bi*1024 + 1024], bank_b(tb)))

                softmax_u(h, 0)
                flush_ao()
                softmax_u(h, 1)
                transp_u(h, 0)
                softmax_u(h, 2)
                transp_u(h, 1)
                softmax_u(h, 3)
                transp_u(h, 2)
                transp_u(h, 3)
                pb = h % 2

                def emit_av(e, kvh=kvh, njt=njt, pb=pb, PTv=PTv):
                    for jt in range(njt):
                        last = e.matmul(bank_f(pb),
                                        VT(jt)[:, kvh*128:(kvh+1)*128],
                                        PTv[:, :, jt*128:(jt+1)*128],
                                        start=(jt == 0), stop=(jt == njt - 1))
                    return last
                rstep("pe", PTRES + ["VT"], [f"bk{pb}"], emit_av)
                pend_ao[0] = (h, pb)
                if not PIPE_AO:
                    flush_ao()

            flush_ao()
            dump(f"AO_{half}", [AOall], [f"AO{i}" for i in range(NHEAD)])
            for dct in range(NB_D):
                wb = wsl(dct % 2)
                rdma("pool", [], [f"wsl{dct%2}"], [(wb, wo_t[dct], {})])
                pb = dct % 2

                def emit_o(e, wb=wb, pb=pb):
                    for ht in range(NHEAD):
                        last = e.matmul(bank_f(pb), wb[:, ht*128:(ht+1)*128],
                                        AO(ht)[:, 0:512],
                                        start=(ht == 0), stop=(ht == NHEAD - 1))
                    return last
                rstep("pe", [f"wsl{dct%2}"] + [f"AO{i}" for i in range(NHEAD)],
                      [f"bk{pb}"], emit_o)
                xo = xin(dct % 2)
                rdma("sp", [], [f"xin{dct%2}"],
                     [(xo[:, 0:512], xall[dct*128:(dct+1)*128, hc:hc+512], {})])
                sl = (dct % 2) * 512
                rstep("dve", [f"bk{pb}", f"xin{dct%2}"], ["stage"],
                      lambda e, xo=xo, pb=pb, sl=sl: e.tensor_add(
                          stage[:, sl:sl+512], bank_f(pb), xo[:, 0:512]))
                st_w = rdma("sp", ["stage"], [f"out_{dct}"],
                            [(out_d[dct*128:(dct+1)*128, half*512:half*512+512],
                              stage[:, sl:sl+512], {})])
                x1_steps[(half, dct)] = st_w

        bar = sch.step("sp", [Step(e, sch.counts[e]) for e in ENGS],
                       lambda e: e.nop())

        dump("x1", [out_d], [f"out_{i}" for i in range(NB_D)])
        for (zdram, z16t, G) in ((zg8_d, z16g, 32), (zu8_d, z16u, 32),
                                 (zd8_d, z16d, 86)):
            zn = f"z16_{G}_{id(z16t)}"
            rdma("sp", [], ["Mb2"], [(zb8, zdram, {})], extra=[bar])

            def emit_zx(e, z16t=z16t, G=G):
                xl = zb8[0:64, :].bitcast(U16).rearrange(
                    "p (a g) -> p a g", g=G // 2)
                xh = zb8[64:128, :].bitcast(U16).rearrange(
                    "p (a g) -> p a g", g=G // 2)
                zl = z16t[0:64, :].rearrange("p (a g) -> p a g", g=G)
                zh = z16t[64:128, :].rearrange("p (a g) -> p a g", g=G)
                e.tensor_scalar(zl[:, :, 0:G//2], xl, 15, None, ALU.bitwise_and)
                e.tensor_scalar(zl[:, :, G//2:G], xl, 8, 15,
                                ALU.logical_shift_right, ALU.bitwise_and)
                e.tensor_scalar(zh[:, :, 0:G//2], xh, 4, 15,
                                ALU.logical_shift_right, ALU.bitwise_and)
                return e.tensor_scalar(zh[:, :, G//2:G], xh, 12, None,
                                       ALU.logical_shift_right)
            rstep("dve", ["Mb2"], [zn], emit_zx)
        ZG = f"z16_32_{id(z16g)}"
        ZU = f"z16_32_{id(z16u)}"
        ZD = f"z16_86_{id(z16d)}"

        ln_stats(lambda dti: out_d[dti*128:(dti+1)*128, :], NB_D,
                 extra_fn=lambda dti: [bar, x1_steps[(0, dti)],
                                       x1_steps[(1, dti)],
                                       res(f"out_{dti}").w])
        ln_bcast(Mb2, Rb2, "Mb2", "Rb2")
        for dti in range(NB_D):
            xi = xin(dti % 2)
            rdma("sp", [f"out_{dti}"], [f"xin{dti%2}"],
                 [(xi, out_d[dti*128:(dti+1)*128, :], {})])
            rstep("pool", [f"xin{dti%2}", "Mb2", "stage"], ["stage"],
                  lambda e, xi=xi: e.tensor_sub(stage, xi, Mb2))
            s = rstep("pool", ["stage", "Rb2"], [XN2[dti]],
                      lambda e, dti=dti: e.tensor_mul(xn2(dti), stage, Rb2))
        ln2_done = s

        dump("xn2", [xn2(i) for i in range(NB_D)], XN2)
        dump("z16", [z16g, z16u, z16d], [ZG, ZU, ZD])

        def dequant_gu(slab_dram, zrun, zrname, srun, srname, a, q, wfi, Wmm,
                       which):
            si = a % 2
            sl = gslab(si)
            rdma("sp", [], [f"gsl{si}"], [(sl, slab_dram[a], {})])

            def emit_ex(e, sl=sl):
                x = sl[0:64, :].bitcast(U16)
                e.tensor_scalar(W16[0:64, 0:2048], x, 15, None, ALU.bitwise_and)
                e.tensor_scalar(W16[64:128, 0:2048], x, 4, 15,
                                ALU.logical_shift_right, ALU.bitwise_and)
                e.tensor_scalar(W16[0:64, 2048:4096], x, 8, 15,
                                ALU.logical_shift_right, ALU.bitwise_and)
                return e.tensor_scalar(W16[64:128, 2048:4096], x, 12, None,
                                       ALU.logical_shift_right)
            rstep("dve", [f"gsl{si}"], ["W16"], emit_ex)
            if which == "u":
                for hf in range(2):
                    def emit_tsu(e, hf=hf, q=q):
                        for i in range(16):
                            g = hf * 16 + i
                            zcol = q * 32 + int(GPOS_32[g])
                            last = e.activation(
                                Wfb[:, i*128:(i+1)*128],
                                W16[:, g*128:(g+1)*128], AF.Identity,
                                scale=sur[:, q*32+g:q*32+g+1],
                                bias=zsur[:, zcol:zcol+1])
                        return last
                    rstep("act", ["W16", "zur", "sur"], WFRES(1), emit_tsu)
                    for bi in range(2):
                        tb = 6 + bi % 2

                        def emit_wtrb(e, bi=bi, tb=tb):
                            for i in range(8):
                                last = e.transpose(
                                    bank_b(tb)[:, i*128:(i+1)*128],
                                    Wfb[:, (bi*8+i)*128:(bi*8+i+1)*128], idtb)
                            return last
                        rstep("pe", WFRES(1) + ["idtb"], [f"bk{tb}"], emit_wtrb)
                        rstep("dve", [f"bk{tb}"], ["Wmm_u"],
                              lambda e, bi=bi, tb=tb, hf=hf: e.tensor_copy(
                                  Wumm[:, hf*2048+bi*1024:hf*2048+bi*1024+1024],
                                  bank_b(tb)))
                return
            for hf in range(2):
                def emit_ts(e, hf=hf, q=q, wfi=wfi, zrun=zrun, srun=srun):
                    for i in range(16):
                        g = hf * 16 + i
                        zcol = q * 32 + int(GPOS_32[g])
                        last = e.tensor_scalar(
                            Wf(wfi)[:, i*128:(i+1)*128],
                            W16[:, g*128:(g+1)*128],
                            zrun[:, zcol:zcol+1], srun[:, q*32+g:q*32+g+1],
                            ALU.subtract, ALU.mult)
                    return last
                rstep("dve", ["W16", zrname, srname], WFRES(wfi), emit_ts)
                for bi in range(4):
                    tb = 6 + bi % 2

                    def emit_wtr(e, bi=bi, tb=tb, wfi=wfi):
                        for i in range(4):
                            last = e.transpose(
                                bank_f(tb)[:, i*128:(i+1)*128],
                                Wf(wfi)[:, (bi*4+i)*128:(bi*4+i+1)*128], idt)
                        return last
                    rstep("pe", WFRES(wfi) + ["idt"], [f"bk{tb}"], emit_wtr)
                    rstep("act", [f"bk{tb}"], [f"Wmm_{which}"],
                          lambda e, bi=bi, tb=tb, hf=hf, Wmm=Wmm: e.copy(
                              Wmm[:, hf*2048+bi*512:hf*2048+bi*512+512],
                              bank_f(tb)))

        sd_off = [0]
        for r, RLEN in enumerate(RUNS):
            L = RLEN * 128
            s0 = RUN_STARTS[r]
            rdma("sp", [], ["sgr", "sur", "sdr", "Mb"],
                 [(sgr[:, 0:RLEN*32], sg_d[:, s0*32:(s0+RLEN)*32], {}),
                  (sur[:, 0:RLEN*32], su_d[:, s0*32:(s0+RLEN)*32], {}),
                  (sdr[:, 0:32*RLEN], sd_d[:, sd_off[0]:sd_off[0]+32*RLEN], {}),
                  (suzr[:, 0:RLEN*32], suz_d[:, s0*32:(s0+RLEN)*32], {})])
            rstep("dve", [ZG, "zgr", "Rb"], ["zgr"],
                  lambda e, s0=s0, RLEN=RLEN: e.tensor_copy(
                      zgr[:, 0:RLEN*32], z16g[:, s0*32:(s0+RLEN)*32]))
            rstep("dve", [ZU, "zur"], ["zur"],
                  lambda e, s0=s0, RLEN=RLEN: e.tensor_copy(
                      zur[:, 0:RLEN*32], z16u[:, s0*32:(s0+RLEN)*32]))
            rstep("dve", ["sur", "sdr"], ["zur"],
                  lambda e, RLEN=RLEN: e.scalar_tensor_tensor(
                      zsur[:, 0:RLEN*32], zur[:, 0:RLEN*32], -1.0,
                      suzr[:, 0:RLEN*32], ALU.mult, ALU.mult))
            sd_off[0] += 32 * RLEN
            pend_h = [None]

            def flush_h():
                if pend_h[0] is not None:
                    qq = pend_h[0]
                    for tc in range(2):
                        rstep("dve", [f"bk{2+tc}", "gsilu"], [f"H{qq}"],
                              lambda e, tc=tc, qq=qq: e.tensor_tensor(
                                  Hb(qq)[:, tc*512:(tc+1)*512], bank_f(2+tc),
                                  gsilu[:, tc*512:(tc+1)*512], ALU.mult))
                    dump(f"h_{r}_{qq}", [Hb(qq)], [f"H{qq}"])
                    pend_h[0] = None

            for q in range(RLEN):
                a = s0 + q
                dequant_gu(g8, zgr, "zgr", sgr, "sgr", a, q, 0, Wgmm, "g")
                dequant_gu(u8w, zur, "zur", sur, "sur", a, q, 1, Wumm, "u")
                flush_h()
                for tc in range(2):
                    def emit_g(e, tc=tc):
                        for g in range(32):
                            last = e.matmul(bank_f(tc),
                                            Wgmm[:, g*128:(g+1)*128],
                                            xn2(g)[:, tc*512:(tc+1)*512],
                                            start=(g == 0), stop=(g == 31))
                        return last
                    rstep("pe", ["Wmm_g"] + XN2, [f"bk{tc}"], emit_g,
                          extra=[ln2_done])
                    rstep("act", [f"bk{tc}"], ["gsilu"],
                          lambda e, tc=tc: e.activation(
                              gsilu[:, tc*512:(tc+1)*512], bank_f(tc), AF.Silu))
                dump(f"gs_{a}", [gsilu], ["gsilu"])
                for tc in range(2):
                    def emit_u(e, tc=tc):
                        for g in range(32):
                            last = e.matmul(bank_f(2+tc),
                                            Wumm[:, g*128:(g+1)*128],
                                            xn2(g)[:, tc*512:(tc+1)*512],
                                            start=(g == 0), stop=(g == 31))
                        return last
                    rstep("pe", ["Wmm_u"] + XN2, [f"bk{2+tc}"], emit_u)
                pend_h[0] = q
                if not PIPE_H:
                    flush_h()
            flush_h()
            pend_d = [None]

            def flush_dstage():
                if pend_d[0] is None:
                    return
                tt2 = pend_d[0]
                si2 = tt2 % 2
                stg = xin(si2)
                for tc in range(2):
                    rstep("dve", [f"bk{4+tc}", f"xin{si2}"], [f"xin{si2}"],
                          lambda e, tc=tc, stg=stg: e.tensor_copy(
                              stg[:, tc*512:(tc+1)*512], bank_f(4+tc)))
                dump(f"dstage_{r}_{tt2}", [stg], [f"xin{si2}"])
                rdma("pool", [f"xin{si2}"], [f"out_{tt2}"],
                     [(out_d[tt2*128:(tt2+1)*128, :], stg,
                       {"accum_op": ALU.add})], extra=[ln2_done])
                pend_d[0] = None

            for t in range(NB_D):
                si = t % 2
                sl = dslab(si)
                rdma("sp", [], [f"dsl{si}"],
                     [(sl[0:64, 0:L], d8[t][:, s0*128:s0*128+L], {})])

                def emit_exd(e, sl=sl, L=L):
                    x = sl[0:64, 0:L].bitcast(U16)
                    e.tensor_scalar(W16[0:64, 0:L//2], x, 15, None,
                                    ALU.bitwise_and)
                    e.tensor_scalar(W16[64:128, 0:L//2], x, 4, 15,
                                    ALU.logical_shift_right, ALU.bitwise_and)
                    e.tensor_scalar(W16[0:64, L//2:L], x, 8, 15,
                                    ALU.logical_shift_right, ALU.bitwise_and)
                    return e.tensor_scalar(W16[64:128, L//2:L], x, 12, None,
                                           ALU.logical_shift_right)
                rstep("dve", [f"dsl{si}"], ["W16"], emit_exd)

                def emit_zfd(e, t=t, s0=s0, RLEN=RLEN, si=si):
                    zfv = zdr(si)[:, 0:RLEN].rearrange(
                        "p (a two) -> p a two", two=2)
                    c_ev = t * 86 + s0 // 2
                    c_od = t * 86 + 43 + s0 // 2
                    e.tensor_copy(zfv[:, :, 0:1],
                                  z16d[:, c_ev:c_ev+RLEN//2].unsqueeze(2))
                    return e.tensor_copy(zfv[:, :, 1:2],
                                         z16d[:, c_od:c_od+RLEN//2].unsqueeze(2))
                rstep("dve", [ZD, f"zdr{si}"], [f"zdr{si}"], emit_zfd)
                for hf in range(2):
                    nblk = RLEN // 2

                    def emit_tsd(e, hf=hf, nblk=nblk, RLEN=RLEN, t=t, si=si):
                        for i in range(nblk):
                            q2 = hf * nblk + i
                            last = e.tensor_scalar(
                                Wf(si)[:, i*128:(i+1)*128],
                                W16[:, q2*128:(q2+1)*128],
                                zdr(si)[:, q2:q2+1],
                                sdr[:, t*RLEN+q2:t*RLEN+q2+1],
                                ALU.subtract, ALU.mult)
                        return last
                    rstep("dve", ["W16", f"zdr{si}", "sdr"], WFRES(si),
                          emit_tsd)
                    nb4 = (nblk + 3) // 4
                    for bi in range(nb4):
                        tb = 6 + bi % 2
                        lo = bi * 4
                        hi2 = min(lo + 4, nblk)

                        def emit_wtrd(e, lo=lo, hi2=hi2, tb=tb, si=si):
                            for i in range(hi2 - lo):
                                last = e.transpose(
                                    bank_f(tb)[:, i*128:(i+1)*128],
                                    Wf(si)[:, (lo+i)*128:(lo+i+1)*128], idt)
                            return last
                        rstep("pe", WFRES(si) + ["idt"], [f"bk{tb}"],
                              emit_wtrd)
                        rstep("act", [f"bk{tb}"], ["Wmm_d"],
                              lambda e, lo=lo, hi2=hi2, tb=tb, hf=hf, nblk=nblk:
                              e.copy(Wdmm[:, (hf*nblk+lo)*128:
                                          (hf*nblk+lo)*128+(hi2-lo)*128],
                                     bank_f(tb)[:, 0:(hi2-lo)*128]))
                dump(f"wd_{r}_{t}", [Wdmm[:, 0:L]], ["Wmm_d"])
                flush_dstage()
                for tc in range(2):
                    def emit_d(e, tc=tc, RLEN=RLEN):
                        for q2 in range(RLEN):
                            last = e.matmul(bank_f(4+tc),
                                            Wdmm[:, q2*128:(q2+1)*128],
                                            Hb(q2)[:, tc*512:(tc+1)*512],
                                            start=(q2 == 0),
                                            stop=(q2 == RLEN - 1))
                        return last
                    rstep("pe", ["Wmm_d"] + [f"H{i}" for i in range(RLEN)],
                          [f"bk{4+tc}"], emit_d)
                pend_d[0] = t
                if not PIPE_D:
                    flush_dstage()
            flush_dstage()
            dump(f"outmid_{r}", [out_d], [f"out_{i}" for i in range(NB_D)])

        sch.step("sp", [Step("pool", sch.counts["pool"]),
                        Step("sp", sch.counts["sp"])], lambda e: e.nop())

        block = stack.enter_context(nc.Block())
        sch.replay(block)
    return nc


_NC_CACHE = []


def kernel(L0_attn_x, L0_attn_wq, L0_attn_wk, L0_attn_wv, L0_attn_wo,
           L0_ffn_gate_qweight, L0_ffn_gate_scales, L0_ffn_gate_qzeros,
           L0_ffn_up_qweight, L0_ffn_up_scales, L0_ffn_up_qzeros,
           L0_ffn_down_qweight, L0_ffn_down_scales, L0_ffn_down_qzeros):
    x = np.asarray(L0_attn_x, dtype=np.float32)
    wq_t = _tile_kpart(np.asarray(L0_attn_wq, dtype=np.float32))
    wk_t = _tile_kpart(np.asarray(L0_attn_wk, dtype=np.float32))
    wv_t = _tile_kpart(np.asarray(L0_attn_wv, dtype=np.float32))
    wo_t = _tile_wo(np.asarray(L0_attn_wo, dtype=np.float32))
    g8 = _gateup_slabs(np.asarray(L0_ffn_gate_qweight))
    u8w = _gateup_slabs(np.asarray(L0_ffn_up_qweight))
    d8 = _down_slabs(np.asarray(L0_ffn_down_qweight))
    sg = _gateup_scales(np.asarray(L0_ffn_gate_scales))
    su = _gateup_scales(np.asarray(L0_ffn_up_scales))
    suz = _gateup_scales_zlayout(np.asarray(L0_ffn_up_scales))
    sd = _down_scales(np.asarray(L0_ffn_down_scales))
    zg8 = _gateup_zero_bytes(np.asarray(L0_ffn_gate_qzeros))
    zu8 = _gateup_zero_bytes(np.asarray(L0_ffn_up_qzeros))
    zd8 = _down_zero_bytes(np.asarray(L0_ffn_down_qzeros))
    idt = np.eye(128, dtype=np.float32)
    idtb = np.eye(128).astype(ml_dtypes.bfloat16)

    xt = np.ascontiguousarray(x.transpose(0, 2, 1))[:, PI_D, :]

    in_maps = []
    core_meta = []
    for c in range(8):
        b, par = c // 2, c % 2
        if par == 0:
            ownX, restH0 = (0, 512), (512, 1024)
            ownY, restH1 = (1536, 2048), (1024, 1536)
        else:
            ownX, restH0 = (512, 1024), (0, 512)
            ownY, restH1 = (1024, 1536), (1536, 2048)
        colord = np.concatenate([np.arange(*ownX), np.arange(*restH0),
                                 np.arange(*ownY), np.arange(*restH1)])
        xc = np.ascontiguousarray(xt[b][:, colord])
        jabs = colord
        maskX = np.empty((4, 128, 1024), dtype=ml_dtypes.bfloat16)
        maskY = np.empty((4, 128, 2048), dtype=ml_dtypes.bfloat16)
        for u in range(4):
            qX = ownX[0] + 128 * u + np.arange(128)[:, None]
            maskX[u] = (jabs[None, :1024] <= qX).astype(ml_dtypes.bfloat16)
            qY = ownY[0] + 128 * u + np.arange(128)[:, None]
            maskY[u] = (jabs[None, :2048] <= qY).astype(ml_dtypes.bfloat16)
        in_maps.append(dict(
            xall=xc, wq_t=wq_t, wk_t=wk_t, wv_t=wv_t, wo_t=wo_t,
            g8=g8, u8w=u8w, d8=d8, sg=sg, su=su, suz=suz, sd=sd,
            zg8=zg8, zu8=zu8, zd8=zd8, maskX=maskX, maskY=maskY,
            idt=idt, idtb=idtb))
        core_meta.append((b, ownX, ownY))

    if not _NC_CACHE:
        _NC_CACHE.append(build_program())
    nc = _NC_CACHE[0]
    resobj = run_bass_kernel_spmd(nc, in_maps, core_ids=list(range(8)))

    out = np.empty((4, 2048, 4096), dtype=np.float32)
    for c in range(8):
        b, ownX, ownY = core_meta[c]
        o = np.asarray(resobj.results[c]["out"])
        out[b, ownX[0]:ownX[1], PI_D] = o[:, 0:512]
        out[b, ownY[0]:ownY[1], PI_D] = o[:, 512:1024]
    return out


